# version 02
# baseline (speedup 1.0000x reference)
import contextlib

import numpy as np
import ml_dtypes
import concourse.bass as bass
import concourse.mybir as mybir
from concourse.bass_utils import run_bass_kernel_spmd

F32 = mybir.dt.float32
BF16 = mybir.dt.bfloat16
I32 = mybir.dt.int32
ALU = mybir.AluOpType
ACT = mybir.ActivationFunctionType
AX = mybir.AxisListType

D = 2048
H = 8192
T_OWN = 1024
EPS_LN = 1e-5
EPS_Q = 1e-5
EPS_H = 1e-5
NSTEPS = 255.0
DEBUG = False
DEBUG2 = False
PHASES_N = None
P1_MODE = 0
P1_QIC = 16
P1_SUB = 99
BISECT = False
SCALE_INLINE = True
SQ_SBUF = True


class Ctx:

    def __init__(self, mode, eng=None, sems=None, marks=None, counts=None):
        self.mode = mode
        self.e = eng
        self.sems = sems
        self.marks = marks
        self.counts = counts

    def val(self, sem, label):
        return self.marks[label][sem]

    def on(self, eng):
        return self.mode == eng

    def wge(self, eng, sem, label):
        if label not in self.marks and BISECT:
            return
        if self.mode == eng:
            self.e.wait_ge(self.sems[sem], self.marks[label][sem])

    def vsep(self, A):
        self.do("vector", lambda e: e.tensor_copy(A["cols"][:, 12:13], A["cols"][:, 13:14]))

    def ssep(self, A):
        self.do("scalar", lambda e: e.copy(A["cols"][:, 12:13], A["cols"][:, 13:14]))

    def prot(self, A, label):
        self.do("vector", lambda e: e.tensor_copy(A["cols"][:, 12:13], A["cols"][:, 13:14]),
                inc="V", label=label)
        self.wge("vector", "V", label)

    def sprot(self, A, label):
        self.do("scalar", lambda e: e.copy(A["cols"][:, 12:13], A["cols"][:, 13:14]),
                inc="S", label=label)
        self.wge("scalar", "S", label)

    def do(self, eng, fn, inc=None, label=None):
        k = 16 if inc in ("dL", "dG") else 1
        if self.mode == "plan":
            if inc is not None:
                self.counts[inc] = self.counts.get(inc, 0) + k
            if label is not None:
                assert label not in self.marks, f"dup label {label}"
                self.marks[label] = dict(self.counts)
            return
        if self.mode == eng:
            ins = fn(self.e)
            if inc is not None:
                ins.then_inc(self.sems[inc], k)


def build_program():
    nc = bass.Bass("TRN2", target_bir_lowering=False, detect_race_conditions=False)

    x_own = nc.dram_tensor("x_own", [T_OWN, D], F32, kind="ExternalInput")
    x_par = nc.dram_tensor("x_par", [T_OWN, D], F32, kind="ExternalInput")
    wq = nc.dram_tensor("wq", [D, D], F32, kind="ExternalInput")
    wk = nc.dram_tensor("wk", [512, D], F32, kind="ExternalInput")
    wv = nc.dram_tensor("wv", [512, D], F32, kind="ExternalInput")
    memd = nc.dram_tensor("memd", [512, 512], F32, kind="ExternalInput")
    memn = nc.dram_tensor("memn", [512, 1], F32, kind="ExternalInput")
    memw = nc.dram_tensor("memw", [1, 1], F32, kind="ExternalInput")
    cfc = nc.dram_tensor("cfc", [D, H], F32, kind="ExternalInput")
    cproj = nc.dram_tensor("cproj", [D, H], F32, kind="ExternalInput")
    out_d = nc.dram_tensor("out", [T_OWN, D], F32, kind="ExternalOutput")
    scr_kind = "ExternalOutput" if DEBUG else "Internal"
    attn_scr = nc.dram_tensor("attn_scr", [T_OWN, D], F32, kind=scr_kind)
    ffn_scr = nc.dram_tensor("ffn_scr", [T_OWN, D], F32, kind=scr_kind)
    if DEBUG:
        dbg_q = nc.dram_tensor("dbg_q", [128, 16 * 1024], BF16, kind="ExternalOutput")
        dbg_k = nc.dram_tensor("dbg_k", [128, 4 * 2048], BF16, kind="ExternalOutput")
        dbg_v = nc.dram_tensor("dbg_v", [128, 16 * 520], BF16, kind="ExternalOutput")
        dbg_c = nc.dram_tensor("dbg_c", [128, 16], F32, kind="ExternalOutput")
        dbg_s = nc.dram_tensor("dbg_s", [1, 8], F32, kind="ExternalOutput")
        dbg_e = nc.dram_tensor("dbg_e", [128, 4 * 1024], BF16, kind="ExternalOutput")
        dbg_h = nc.dram_tensor("dbg_h", [128, 128], F32, kind="ExternalOutput")
        dbg_xn = nc.dram_tensor("dbg_xn", [128, 16 * 1024], BF16, kind="ExternalOutput")
        dbg_pv = nc.dram_tensor("dbg_pv", [128, 1024], BF16, kind="ExternalOutput")
        dbg_rd = nc.dram_tensor("dbg_rd", [128, 8], F32, kind="ExternalOutput")
        dbg_dt = nc.dram_tensor("dbg_dt", [128, 8], F32, kind="ExternalOutput")
        dbg_pt = nc.dram_tensor("dbg_pt", [128, 2048], BF16, kind="ExternalOutput")
    if DEBUG2:
        dbg2_pv = nc.dram_tensor("dbg2_pv", [65, 32 * 1024], BF16, kind="ExternalOutput")
        dbg2_am = nc.dram_tensor("dbg2_am", [128, 32 * 512], BF16, kind="ExternalOutput")
        dbg2_rd = nc.dram_tensor("dbg2_rd", [128, 32 * 8], F32, kind="ExternalOutput")
        dbg2_rnw = nc.dram_tensor("dbg2_rnw", [128, 8 * 4], F32, kind="ExternalOutput")
        dbg2_nrow = nc.dram_tensor("dbg2_nrow", [128, 8 * 512], BF16, kind="ExternalOutput")
        dbg2_el = nc.dram_tensor("dbg2_el", [128, 4 * 4096], BF16, kind="ExternalOutput")
        dbg2_el2 = nc.dram_tensor("dbg2_el2", [128, 4 * 4096], BF16, kind="ExternalOutput")
        dbg2_sm1 = nc.dram_tensor("dbg2_sm1", [128, 2048], BF16, kind="ExternalOutput")
        dbg2_sm2 = nc.dram_tensor("dbg2_sm2", [128, 2048], BF16, kind="ExternalOutput")
        dbg2_smr = nc.dram_tensor("dbg2_smr", [128, 512], BF16, kind="ExternalOutput")

    ident_bf_d = nc.inline_tensor(np.eye(128, dtype=np.float32).astype(ml_dtypes.bfloat16), name="ib_d")
    ident_f32_d = nc.inline_tensor(np.eye(128, dtype=np.float32), name="if_d")
    onesrow_d = nc.inline_tensor(np.ones((1, 128), dtype=np.float32), name="or_d")

    A = {}
    SB_BASE = 16544

    def at(name, off, shape, dtype):
        A[name] = nc.alloc_sbuf_tensor_at(name, shape, dtype, offset=SB_BASE + off)

    at("ident_bf", 0, [128, 128], BF16)
    at("ident_f32", 256, [128, 128], F32)
    at("onesrow", 768, [1, 128], F32)
    at("onescol", 1280, [128, 1], F32)
    at("ones2", 1312, [128, 2], BF16)
    at("smem", 1344, [128, 4, 512], BF16)
    at("smemn", 5440, [128, 4, 1], BF16)
    at("cols", 5472, [128, 16], F32)
    at("sc1", 5536, [1, 8], F32)
    at("nq_accf", 5568, [1, 32], F32)
    at("nk_accf", 5696, [1, 16], F32)
    at("sc_cfc", 5760, [128, 64], F32)
    at("sc_cp", 6016, [128, 64], F32)
    at("m_row", 6272, [1, 1024], F32)
    at("m_bcast", 10368, [128, 1024], F32)
    at("rmax_tok", 14464, [128, 8], F32)
    at("oscale", 14496, [128, 8], F32)
    at("stats", 14528, [128, 8], F32)
    at("msb8", 14560, [8, 128], F32)
    at("rnw", 15072, [128, 4], F32)
    at("rdw", 15104, [128, 4], F32)
    at("dtmp", 15136, [128, 8], F32)
    at("rdw8", 15168, [128, 8], F32)
    at("shist", 15232, [128, 128], F32)
    at("memnrow", 15744, [1, 512], F32)
    at("smemn_rep", 17792, [128, 4, 128], BF16)
    at("onesT", 18816, [128, 128], F32)

    at("xn_own", 20480, [128, 16, 1024], BF16)

    at("xn_par", 53248, [128, 16, 1024], BF16)
    at("q_fm", 86016, [128, 16, 1024], BF16)
    at("k_fm", 118784, [128, 4, 2048], BF16)
    at("v_tm", 135168, [128, 16, 520], BF16)
    at("ptA", 151808, [128, 2048], BF16)
    at("ptB", 155904, [128, 2048], BF16)
    at("eluq", 160000, [128, 4, 1024], BF16)
    at("v_sb", 168192, [128, 512], BF16)
    at("stage", 168704, [128, 4, 512], F32)
    at("pv_sb", 176896, [128, 1024], BF16)
    at("am_sb", 178944, [128, 512], BF16)
    at("nrow_sb", 179968, [128, 512], BF16)
    at("sq_sb", 180992, [128, 1024], BF16)
    at("h_st", 53248, [128, 64, 1024], BF16)
    at("cmb_x", 53248, [128, 2, 2048], F32)
    at("cmb_a", 86016, [128, 2, 2048], F32)
    at("cmb_f", 118784, [128, 2, 2048], F32)
    at("cmb_o", 151808, [128, 2, 2048], F32)

    at("x_tile", 184320, [128, 2, 2048], F32)
    at("ln_junk", 200704, [128, 2048], F32)
    at("w_buf", 184320, [128, 2, 2048], F32)
    at("wT_buf", 200704, [128, 16, 128], BF16)
    at("scale_buf", 184320, [128, 2, 2048], F32)
    at("cfc_buf", 184320, [128, 2, 16, 128], F32)
    at("q1_buf", 200704, [128, 2, 16, 128], BF16)
    at("hmax", 208896, [128, 1024], BF16)
    at("sqh", 200704, [128, 1024], F32)
    at("i32h", 204800, [128, 1024], I32)
    at("cp_buf", 184320, [128, 2, 512], F32)
    at("q2_sl", 188416, [128, 512], BF16)
    at("w2T", 189440, [128, 64, 128], BF16)
    at("ffn_sb", 205824, [128, 2, 512], F32)
    at("o_sb", 209920, [128, 512], BF16)

    es = contextlib.ExitStack()
    with es:
        sems = {n: es.enter_context(nc.semaphore(n)) for n in ("dL", "dG", "T", "V", "S")}
        marks, counts = {}, {}
        ps = {}
        for name, shape, dt in (("pa", [128, 1024], F32), ("pb", [128, 1024], F32),
                                ("pc", [128, 512], F32), ("pd", [128, 512], F32),
                                ("pe", [128, 512], F32), ("pf", [128, 1024], BF16)):
            ps[name] = es.enter_context(nc.psum_tensor(name, shape, dt))

        IB = A["ident_bf"]
        IF = A["ident_f32"]

        def ph_setup(c: Ctx):
            c.do("sync", lambda e: e.dma_start(A["ident_bf"][:], ident_bf_d[:]), inc="dL", label="su_ib")
            c.do("sync", lambda e: e.dma_start(A["ident_f32"][:], ident_f32_d[:]), inc="dL", label="su_if")
            c.do("sync", lambda e: e.dma_start(A["onesrow"][:], onesrow_d[:]), inc="dL", label="su_or")
            for fc in range(4):
                c.do("sync", lambda e, fc=fc: e.dma_start(
                    A["scale_buf"][:, fc // 2, (fc % 2) * 512:(fc % 2) * 512 + 512],
                    memd[fc * 128:(fc + 1) * 128, :]), inc="dL", label=f"su_md{fc}")
            c.do("sync", lambda e: e.dma_start(A["sc1"][0:1, 0:1], memw[:, :]), inc="dL", label="su_mw")
            c.do("sync", lambda e: e.dma_start(
                A["rdw"][:, 0:4], bass.AP(memn, 0, [[1, 128], [128, 4]]),
                allow_slow_non_contiguous=True), inc="dL", label="su_mn")
            c.do("sync", lambda e: e.dma_start(
                A["memnrow"][0:1, :], bass.AP(memn, 0, [[512, 1], [1, 512]])),
                inc="dL", label="su_mnr")

            c.do("vector", lambda e: e.memset(A["cols"][:], 0.0))
            c.do("vector", lambda e: e.memset(A["sc1"][0:1, 1:8], 0.0))
            c.do("vector", lambda e: e.memset(A["ones2"][:], 0.0))
            c.do("vector", lambda e: e.memset(A["ones2"][0:64, 0:1], 1.0))
            c.do("vector", lambda e: e.memset(A["ones2"][64:128, 1:2], 1.0))
            c.do("vector", lambda e: e.memset(A["onescol"][:], 1.0))
            c.do("vector", lambda e: e.memset(A["onesT"][:], 1.0))
            c.do("vector", lambda e: e.memset(A["hmax"][:], 0.0))
            c.do("vector", lambda e: e.memset(A["sc_cfc"][:], 0.0))
            c.do("vector", lambda e: e.memset(A["sc_cp"][:], 0.0))
            c.do("vector", lambda e: e.memset(
                bass.AP(A["v_tm"], 64, [[16 * 520, 128], [520, 16], [65, 8]]), 1.0))
            c.wge("vector", "dL", "su_mnr")
            for fc in range(4):
                c.do("vector", lambda e, fc=fc: e.tensor_copy(
                    A["smem"][:, fc, :],
                    A["scale_buf"][:, fc // 2, (fc % 2) * 512:(fc % 2) * 512 + 512]))
            c.do("vector", lambda e: e.tensor_copy(
                bass.AP(A["smemn"], 0, [[4, 128], [1, 4]]), A["rdw"][:, 0:4]),
                inc="V", label="su_v")
            c.wge("scalar", "dL", "su_mnr")
            c.do("scalar", lambda e: e.activation(A["sc1"][0:1, 1:2], A["sc1"][0:1, 0:1], ACT.Sigmoid),
                 inc="S", label="su_sig")
            c.wge("vector", "S", "su_sig")
            c.do("vector", lambda e: e.tensor_scalar(A["sc1"][0:1, 2:3], A["sc1"][0:1, 1:2],
                                                     -1.0, 1.0, ALU.mult, ALU.add),
                 inc="V", label="su_ow")
            c.wge("tensor", "V", "su_ow")
            c.wge("tensor", "dL", "su_or")
            c.wge("tensor", "dL", "su_mnr")
            for fc in range(4):
                c.do("tensor", lambda e, fc=fc: e.matmul(
                    ps["pc"][:, fc * 128:(fc + 1) * 128], A["memnrow"][0:1, fc * 128:(fc + 1) * 128],
                    A["onesrow"][:, :], start=True, stop=True),
                    inc="T" if fc == 3 else None, label="su_rep" if fc == 3 else None)
            c.wge("vector", "T", "su_rep")
            c.do("vector", lambda e: e.tensor_copy(
                A["smemn_rep"][:, :, :].rearrange("p a b -> p (a b)"), ps["pc"][:, 0:512]),
                inc="V", label="su_repv")
            c.do("tensor", lambda e: e.matmul(ps["pe"][:, 0:1], A["onesrow"][:, :],
                                              A["sc1"][0:1, 1:2], start=True, stop=True))
            c.do("tensor", lambda e: e.matmul(ps["pe"][:, 1:2], A["onesrow"][:, :],
                                              A["sc1"][0:1, 2:3], start=True, stop=True),
                 inc="T", label="su_bc")
            c.wge("vector", "T", "su_bc")
            c.do("vector", lambda e: e.tensor_copy(A["cols"][:, 0:2], ps["pe"][:, 0:2]))
            c.vsep(A)
            c.do("vector", lambda e: e.tensor_scalar(
                A["cols"][:, 14:15], A["cols"][:, 1:2], 1099511627776.0, None, ALU.mult),
                 inc="V", label="su_cols")

        def ph_p0(c: Ctx):
            for i in range(16):
                own = i < 8
                buf = i % 2
                src = x_own if own else x_par
                r0 = (i % 8) * 128
                if i < 2:
                    c.wge("sync", "V", "su_v")
                else:
                    c.wge("sync", "S", f"p0_ln_{i-2}")
                c.do("sync", lambda e, src=src, r0=r0, buf=buf: e.dma_start(
                    A["x_tile"][:, buf, :], src[r0:r0 + 128, :]), inc="dL", label=f"p0_ld_{i}")
                c.wge("vector", "dL", f"p0_ld_{i}")
                c.do("vector", lambda e, buf=buf: e.reduce_sum(
                    A["stats"][:, 0:1], A["x_tile"][:, buf, :], axis=AX.X), inc="V",
                    label=f"p0_sx_{i}")
                c.prot(A, f"p0_sxp_{i}")
                c.do("vector", lambda e, buf=buf: e.scalar_tensor_tensor(
                    A["ln_junk"][:], A["x_tile"][:, buf, :], 1.0, A["x_tile"][:, buf, :],
                    ALU.mult, ALU.mult, accum_out=A["stats"][:, 1:2]),
                    inc="V", label=f"p0_sxx_{i}")
                c.prot(A, f"p0_sxxp_{i}")
                c.do("vector", lambda e: e.tensor_scalar(A["stats"][:, 2:3], A["stats"][:, 0:1],
                                                         1.0 / D, None, ALU.mult))
                c.vsep(A)
                c.do("vector", lambda e: e.tensor_mul(A["stats"][:, 3:4], A["stats"][:, 2:3],
                                                      A["stats"][:, 2:3]))
                c.vsep(A)
                c.do("vector", lambda e: e.scalar_tensor_tensor(
                    A["stats"][:, 4:5], A["stats"][:, 1:2], 1.0 / D, A["stats"][:, 3:4],
                    ALU.mult, ALU.subtract))
                c.vsep(A)
                c.do("vector", lambda e: e.tensor_scalar(A["stats"][:, 4:5], A["stats"][:, 4:5],
                                                         EPS_LN, None, ALU.add),
                     inc="V", label=f"p0_var_{i}")
                c.wge("scalar", "V", f"p0_var_{i}")
                c.do("scalar", lambda e: e.activation(A["stats"][:, 5:6], A["stats"][:, 4:5],
                                                      ACT.Sqrt), inc="S", label=f"p0_sd_{i}")
                c.wge("vector", "S", f"p0_sd_{i}")
                c.do("vector", lambda e: e.reciprocal(A["stats"][:, 6:7], A["stats"][:, 5:6]))
                c.prot(A, f"p0_rcp_{i}")
                c.do("vector", lambda e: e.scalar_tensor_tensor(
                    A["stats"][:, 7:8], A["stats"][:, 2:3], -1.0, A["stats"][:, 6:7],
                    ALU.mult, ALU.mult), inc="V", label=f"p0_rb_{i}")
                c.wge("scalar", "V", f"p0_rb_{i}")
                if i >= 1:
                    c.wge("scalar", "T", f"p0_tr_{i-1}_3")
                c.do("scalar", lambda e, buf=buf: e.activation(
                    A["ptA"][:, 0:2048], A["x_tile"][:, buf, :], ACT.Identity,
                    bias=A["stats"][:, 7:8], scale=A["stats"][:, 6:7]),
                    inc="S", label=f"p0_ln_{i}")
                if DEBUG:
                    c.wge("vector", "S", f"p0_ln_{i}")
                    c.do("vector", lambda e, i=i: e.tensor_copy(
                        A["shist"][:, i * 8:(i + 1) * 8], A["stats"][:, 0:8]))
                dst = A["xn_own"] if own else A["xn_par"]
                tcol = (i % 8) * 128
                for grp in range(4):
                    if grp == 0:
                        c.wge("tensor", "S", f"p0_ln_{i}")
                        if i == 0:
                            c.wge("tensor", "dL", "su_mnr")
                        else:
                            c.wge("tensor", "V", f"p0_ev_{i-1}_3")
                    else:
                        c.wge("tensor", "V", f"p0_ev_{i}_{grp-1}")
                    for dd in range(4):
                        d = grp * 4 + dd
                        c.do("tensor", lambda e, d=d, dd=dd: e.transpose(
                            ps["pf"][:, dd * 128:(dd + 1) * 128],
                            A["ptA"][:, d * 128:(d + 1) * 128], IB[:]),
                            inc="T" if dd == 3 else None,
                            label=f"p0_tr_{i}_{grp}" if dd == 3 else None)
                    c.wge("vector", "T", f"p0_tr_{i}_{grp}")
                    for dd in range(4):
                        d = grp * 4 + dd
                        c.do("vector", lambda e, d=d, dd=dd, dst=dst, tcol=tcol: e.tensor_copy(
                            dst[:, d, tcol:tcol + 128], ps["pf"][:, dd * 128:(dd + 1) * 128]),
                            inc="V" if dd == 3 else None,
                            label=f"p0_ev_{i}_{grp}" if dd == 3 else None)

        def w_pipeline(c: Ctx, wsrc, n_ic, kind, prev_pf_label, prev_wbuf_waits):
            last_mm = {"q": lambda ic: f"p1q_mm_{ic}_1",
                       "k": lambda ic: f"p1k_mm_{ic}_1_1",
                       "v": lambda ic: f"p1v_mm_{ic}_1_1"}[kind]
            for ic in range(n_ic):
                buf = ic % 2
                if ic < 2:
                    for (weng, wsem, wlab) in prev_wbuf_waits:
                        c.wge("sync", wsem, wlab)
                else:
                    c.wge("sync", "V", f"p1{kind}_cast_{ic-2}")
                c.do("sync", lambda e, ic=ic, buf=buf, wsrc=wsrc: e.dma_start(
                    A["w_buf"][:, buf, :], wsrc[ic * 128:(ic + 1) * 128, :]),
                    inc="dL", label=f"p1{kind}_ld_{ic}")
                c.wge("vector", "dL", f"p1{kind}_ld_{ic}")
                c.do("vector", lambda e, buf=buf: e.tensor_copy(
                    A["ptB"][:, 0:2048], A["w_buf"][:, buf, :]),
                    inc="V", label=f"p1{kind}_cast_{ic}")
                if P1_SUB < 2:
                    continue
                for grp in range(4):
                    if grp == 0:
                        c.wge("tensor", "V", f"p1{kind}_cast_{ic}")
                        c.wge("tensor", "V", prev_pf_label if ic == 0 else f"p1{kind}_wev_{ic-1}_3")
                    else:
                        c.wge("tensor", "V", f"p1{kind}_wev_{ic}_{grp-1}")
                    for dd in range(4):
                        d = grp * 4 + dd
                        c.do("tensor", lambda e, d=d, dd=dd: e.transpose(
                            ps["pf"][:, dd * 128:(dd + 1) * 128],
                            A["ptB"][:, d * 128:(d + 1) * 128], IB[:]),
                            inc="T" if dd == 3 else None,
                            label=f"p1{kind}_wtr_{ic}_{grp}" if dd == 3 else None)
                    c.wge("vector", "T", f"p1{kind}_wtr_{ic}_{grp}")
                    if grp == 0 and ic >= 1:
                        c.wge("vector", "T", last_mm(ic - 1))
                    for dd in range(4):
                        d = grp * 4 + dd
                        c.do("vector", lambda e, d=d, dd=dd: e.tensor_copy(
                            A["wT_buf"][:, d, :], ps["pf"][:, dd * 128:(dd + 1) * 128]),
                            inc="V" if dd == 3 else None,
                            label=f"p1{kind}_wev_{ic}_{grp}" if dd == 3 else None)
                if P1_SUB < 3:
                    continue
                halves = [("own", A["xn_own"])] if kind == "q" else \
                         [("own", A["xn_own"]), ("par", A["xn_par"])]
                for hi, (hn, xsrc) in enumerate(halves):
                    for tch in range(2):
                        pdst = ps["pc"] if tch == 0 else ps["pd"]
                        mlabel = (f"p1q_mm_{ic}_{tch}" if kind == "q"
                                  else f"p1{kind}_mm_{ic}_{hi}_{tch}")
                        for d in range(16):
                            if d == 0 and hi == 0 and tch == 0:
                                c.wge("tensor", "V", f"p1{kind}_wev_{ic}_3")
                            if d == 0:
                                prev = None
                                if kind == "q" and ic > 0:
                                    prev = f"p1q_nmm_{ic-1}_{tch}"
                                if kind != "q" and (ic > 0 or hi > 0):
                                    ph_, pic = (hi - 1, ic) if hi > 0 else (1, ic - 1)
                                    prev = (f"p1k_nmm_{pic}_{ph_}_{tch}" if kind == "k"
                                            else f"p1v_evd_{pic}_{ph_}_{tch}")
                                if prev is not None:
                                    c.wge("tensor", "T", prev)
                            c.do("tensor", lambda e, d=d, tch=tch, pdst=pdst, xsrc=xsrc: e.matmul(
                                pdst[:, 0:512], A["wT_buf"][:, d, :],
                                xsrc[:, d, tch * 512:(tch + 1) * 512],
                                start=(d == 0), stop=(d == 15)),
                                inc="T" if d == 15 else None,
                                label=mlabel if d == 15 else None)
                        if P1_SUB < 4:
                            continue
                        if kind == "q":
                            c.wge("vector", "T", mlabel)
                            c.do("vector", lambda e, ic=ic, tch=tch, pdst=pdst: e.tensor_copy(
                                A["q_fm"][:, ic, tch * 512:(tch + 1) * 512], pdst[:, 0:512]),
                                inc="V", label=f"p1q_ev_{ic}_{tch}")
                            if P1_SUB < 5:
                                continue
                            if SQ_SBUF:
                                c.wge("scalar", "V", f"p1q_ev_{ic}_{tch}")
                                c.do("scalar", lambda e, ic=ic, tch=tch: e.activation(
                                    A["sq_sb"][:, tch * 512:(tch + 1) * 512],
                                    A["q_fm"][:, ic, tch * 512:(tch + 1) * 512],
                                    ACT.Square), inc="S", label=f"p1q_sq_{ic}_{tch}")
                            else:
                                c.wge("scalar", "T", mlabel)
                                c.do("scalar", lambda e, tch=tch, pdst=pdst: e.activation(
                                    A["sq_sb"][:, tch * 512:(tch + 1) * 512], pdst[:, 0:512],
                                    ACT.Square), inc="S", label=f"p1q_sq_{ic}_{tch}")
                            if P1_SUB < 6:
                                continue
                            c.wge("tensor", "S", f"p1q_sq_{ic}_{tch}")
                            c.wge("tensor", "V", f"p1q_ev_{ic}_{tch}")
                            if ic > 0 or tch > 0:
                                c.wge("tensor", "V",
                                      f"p1q_nred_{ic}_{tch-1}" if tch > 0 else f"p1q_nred_{ic-1}_1")
                            c.do("tensor", lambda e, tch=tch: e.matmul(
                                ps["pa"][0:1, 0:512], A["ones2"][:, 0:1],
                                A["sq_sb"][:, tch * 512:(tch + 1) * 512], start=True, stop=True))
                            c.do("tensor", lambda e, tch=tch: e.matmul(
                                ps["pa"][0:1, 512:1024], A["ones2"][:, 1:2],
                                A["sq_sb"][:, tch * 512:(tch + 1) * 512], start=True, stop=True),
                                inc="T", label=f"p1q_nmm_{ic}_{tch}")
                            if P1_SUB >= 7:
                                c.wge("vector", "T", f"p1q_nmm_{ic}_{tch}")
                                c.do("vector", lambda e, ic=ic, tch=tch: e.reduce_max(
                                    A["nq_accf"][0:1, 2 * ic + tch:2 * ic + tch + 1],
                                    ps["pa"][0:1, 0:1024], axis=AX.X),
                                    inc="V", label=f"p1q_nred_{ic}_{tch}")
                        elif kind == "k":
                            col = hi * 1024 + tch * 512
                            c.wge("vector", "T", mlabel)
                            c.do("vector", lambda e, ic=ic, col=col, pdst=pdst: e.tensor_copy(
                                A["k_fm"][:, ic, col:col + 512], pdst[:, 0:512]),
                                inc="V", label=f"p1k_ev_{ic}_{hi}_{tch}")
                            c.wge("scalar", "V", f"p1k_ev_{ic}_{hi}_{tch}")
                            c.do("scalar", lambda e, ic=ic, col=col, tch=tch: e.activation(
                                A["sq_sb"][:, tch * 512:(tch + 1) * 512],
                                A["k_fm"][:, ic, col:col + 512],
                                ACT.Square), inc="S", label=f"p1k_sq_{ic}_{hi}_{tch}")
                            c.wge("tensor", "S", f"p1k_sq_{ic}_{hi}_{tch}")
                            c.wge("tensor", "V", f"p1k_ev_{ic}_{hi}_{tch}")
                            c.wge("tensor", "V", "p1q_nred_15_1" if (ic == 0 and hi == 0 and tch == 0)
                                  else (f"p1k_nred_{ic}_{hi}_{tch-1}" if tch > 0
                                        else (f"p1k_nred_{ic}_{hi-1}_1" if hi > 0
                                              else f"p1k_nred_{ic-1}_1_1")))
                            c.do("tensor", lambda e, tch=tch: e.matmul(
                                ps["pa"][0:1, 0:512], A["ones2"][:, 0:1],
                                A["sq_sb"][:, tch * 512:(tch + 1) * 512], start=True, stop=True))
                            c.do("tensor", lambda e, tch=tch: e.matmul(
                                ps["pa"][0:1, 512:1024], A["ones2"][:, 1:2],
                                A["sq_sb"][:, tch * 512:(tch + 1) * 512], start=True, stop=True),
                                inc="T", label=f"p1k_nmm_{ic}_{hi}_{tch}")
                            c.wge("vector", "T", f"p1k_nmm_{ic}_{hi}_{tch}")
                            c.do("vector", lambda e, ic=ic, hi=hi, tch=tch: e.reduce_max(
                                A["nk_accf"][0:1, 4 * ic + 2 * hi + tch:4 * ic + 2 * hi + tch + 1],
                                ps["pa"][0:1, 0:1024], axis=AX.X),
                                inc="V", label=f"p1k_nred_{ic}_{hi}_{tch}")
                        else:
                            c.wge("vector", "T", mlabel)
                            if ic > 0 or hi > 0 or tch > 0:
                                prev_vtr = (f"p1v_vtr_{ic}_{hi}_{tch-1}" if tch > 0
                                            else (f"p1v_vtr_{ic}_{hi-1}_1" if hi > 0
                                                  else f"p1v_vtr_{ic-1}_1_1"))
                                c.wge("vector", "T", prev_vtr)
                            c.do("vector", lambda e, pdst=pdst: e.tensor_copy(
                                A["v_sb"][:, 0:512], pdst[:, 0:512]),
                                inc="V", label=f"p1v_evd_{ic}_{hi}_{tch}")
                            c.wge("tensor", "V", f"p1v_evd_{ic}_{hi}_{tch}")
                            c.wge("tensor", "V", f"p1v_vev_{ic}_{hi}_{tch-1}" if tch > 0
                                  else (f"p1v_vev_{ic}_{hi-1}_1" if hi > 0
                                        else (f"p1v_vev_{ic-1}_1_1" if ic > 0
                                              else f"p1v_wev_{0}_3")))
                            for tt in range(4):
                                c.do("tensor", lambda e, tt=tt: e.transpose(
                                    ps["pf"][:, tt * 128:(tt + 1) * 128],
                                    A["v_sb"][:, tt * 128:(tt + 1) * 128], IB[:]),
                                    inc="T" if tt == 3 else None,
                                    label=f"p1v_vtr_{ic}_{hi}_{tch}" if tt == 3 else None)
                            c.wge("vector", "T", f"p1v_vtr_{ic}_{hi}_{tch}")
                            tok0 = (hi * 1024 + tch * 512) // 128
                            h0, h1 = 2 * ic, 2 * ic + 1
                            for tt in range(4):
                                tb = tok0 + tt
                                c.do("vector", lambda e, tt=tt, tb=tb, h0=h0: e.tensor_copy(
                                    A["v_tm"][:, tb, h0 * 65:h0 * 65 + 64],
                                    ps["pf"][:, tt * 128:tt * 128 + 64]))
                                c.do("vector", lambda e, tt=tt, tb=tb, h1=h1: e.tensor_copy(
                                    A["v_tm"][:, tb, h1 * 65:h1 * 65 + 64],
                                    ps["pf"][:, tt * 128 + 64:tt * 128 + 128]),
                                    inc="V" if tt == 3 else None,
                                    label=f"p1v_vev_{ic}_{hi}_{tch}" if tt == 3 else None)

        def ph_p1(c: Ctx):
            w_pipeline(c, wq, P1_QIC, "q", "p0_ev_15_3",
                       [("sync", "S", "p0_ln_15"), ("sync", "V", "p0_sx_15")])
            if P1_MODE >= 2 or P1_MODE == 0:
                w_pipeline(c, wk, 4, "k", f"p1q_wev_{P1_QIC-1}_3", [("sync", "V", f"p1q_cast_{P1_QIC-1}")])
            if P1_MODE >= 3 or P1_MODE == 0:
                w_pipeline(c, wv, 4, "v", "p1k_wev_3_3", [("sync", "V", "p1k_cast_3")])
            if P1_MODE not in (0, 4):
                return
            c.wge("vector", "V", "p1v_vev_3_1_1")
            c.prot(A, "p1_cpre")
            c.do("vector", lambda e: e.reduce_max(A["sc1"][0:1, 3:4], A["nq_accf"][0:1, :], axis=AX.X))
            c.do("vector", lambda e: e.reduce_max(A["sc1"][0:1, 5:6], A["nk_accf"][0:1, :], axis=AX.X))
            c.prot(A, "p1_cprot")
            c.do("vector", lambda e: e.tensor_mul(A["sc1"][0:1, 7:8], A["sc1"][0:1, 3:4],
                                                  A["sc1"][0:1, 5:6]), inc="V", label="p1_c0")
            c.wge("scalar", "V", "p1_c0")
            c.do("scalar", lambda e: e.activation(A["sc1"][0:1, 7:8], A["sc1"][0:1, 7:8], ACT.Sqrt),
                 inc="S", label="p1_c1")
            c.wge("vector", "S", "p1_c1")
            c.do("vector", lambda e: e.tensor_scalar(A["sc1"][0:1, 7:8], A["sc1"][0:1, 7:8],
                                                     -0.125 * 1.02, None, ALU.mult),
                 inc="V", label="p1_c2")
            c.wge("tensor", "V", "p1_c2")
            c.do("tensor", lambda e: e.matmul(ps["pe"][:, 2:3], A["onesrow"][:, :],
                                              A["sc1"][0:1, 7:8], start=True, stop=True),
                 inc="T", label="p1_c3")
            c.wge("vector", "T", "p1_c3")
            c.do("vector", lambda e: e.tensor_copy(A["cols"][:, 2:3], ps["pe"][:, 2:3]),
                 inc="V", label="p1_bias")

        def ph_p2(c: Ctx):
            def scale_chunk(cn):
                srcw = cfc if cn < 64 else cproj
                k = cn % 64
                r, cc = k // 4, k % 4
                buf = cn % 2
                acc = A["sc_cfc"] if cn < 64 else A["sc_cp"]
                if cn < 2:
                    c.wge("sync", "V", "p1v_cast_3")
                else:
                    c.wge("sync", "V", f"p2sc_red_{cn-2}")
                c.do("sync", lambda e, srcw=srcw, r=r, cc=cc, buf=buf: e.dma_start(
                    A["scale_buf"][:, buf, :],
                    srcw[r * 128:(r + 1) * 128, cc * 2048:(cc + 1) * 2048]),
                    inc="dL", label=f"p2sc_ld_{cn}")
                c.wge("vector", "dL", f"p2sc_ld_{cn}")
                c.do("vector", lambda e, buf=buf, acc=acc, k=k: e.tensor_reduce(
                    acc[:, k:k + 1], A["scale_buf"][:, buf, :], axis=AX.X, op=ALU.add,
                    apply_absolute_value=True), inc="V", label=f"p2sc_red_{cn}")

            for u in range(8):
                g, sc = u // 2, u % 2
                if SCALE_INLINE:
                    for jj in range(16):
                        scale_chunk(u * 16 + jj)
                if sc == 0:
                    for fc in range(4):
                        qt = g * 4 + fc
                        if g >= 1 and fc == 0:
                            c.wge("vector", "T", f"p2_am_{2*g-1}_3")
                            c.wge("vector", "T", f"p2_pv_{2*g-1}_3_1")
                            if DEBUG2:
                                c.wge("vector", "dG", f"p2_eldmp_{g-1}")
                        c.do("vector", lambda e, qt=qt: e.tensor_scalar(
                            A["ptB"][:, 0:1024], A["q_fm"][:, qt, :], 0.0, None, ALU.min),
                            inc="V", label=f"p2_elmin_{g}_{fc}")
                        c.wge("scalar", "V", f"p2_elmin_{g}_{fc}")
                        c.do("scalar", lambda e: e.activation(
                            A["ptA"][:, 0:1024], A["ptB"][:, 0:1024], ACT.Exp),
                            inc="S", label=f"p2_elexp_{g}_{fc}")
                        c.wge("vector", "S", f"p2_elexp_{g}_{fc}")
                        c.do("vector", lambda e, qt=qt, fc=fc: e.scalar_tensor_tensor(
                            A["eluq"][:, fc, :], A["q_fm"][:, qt, :], 0.0, A["ptA"][:, 0:1024],
                            ALU.max, ALU.add))
                        c.vsep(A)
                        c.do("vector", lambda e, fc=fc: e.tensor_scalar(
                            A["eluq"][:, fc, :], A["eluq"][:, fc, :], -1.0, None, ALU.add),
                            inc="V", label=f"p2_el_{g}_{fc}")
                if DEBUG2 and sc == 0:
                    c.wge("gpsimd", "V", f"p2_el_{g}_3")
                    c.do("gpsimd", lambda e, g=g: e.dma_start(
                        dbg2_el[:, g * 4096:(g + 1) * 4096],
                        A["eluq"][:, :, :].rearrange("p a b -> p (a b)")), inc="dG")
                c.wge("tensor", "V", f"p2_el_{g}_3")
                c.wge("tensor", "V", "su_repv" if u == 0 else f"p2_amev_{u-1}_3")
                for fc in range(4):
                    c.do("tensor", lambda e, fc=fc, sc=sc: e.matmul(
                        ps["pe"][:, 0:512], A["smemn_rep"][:, fc, :],
                        A["eluq"][:, fc, sc * 512:(sc + 1) * 512],
                        start=(fc == 0), stop=(fc == 3)),
                        inc="T" if fc == 3 else None, label=f"p2_nm_{u}" if fc == 3 else None)
                c.wge("vector", "T", f"p2_nm_{u}")
                c.do("vector", lambda e: e.tensor_copy(A["nrow_sb"][:, :], ps["pe"][:, 0:512]))
                c.do("vector", lambda e: e.tensor_copy(A["cols"][:, 12:13], A["cols"][:, 13:14]),
                     inc="V", label=f"p2_nev_{u}")
                c.wge("tensor", "V", f"p2_nev_{u}")
                if u >= 1:
                    c.wge("tensor", "V", f"p2_ambl_{u-1}_3")
                elif True:
                    c.wge("tensor", "V", "p1_bias")
                for tc in range(4):
                    c.do("tensor", lambda e, tc=tc: e.transpose(
                        ps["pf"][:, tc * 128:(tc + 1) * 128],
                        A["nrow_sb"][:, tc * 128:(tc + 1) * 128], IB[:]),
                        inc="T" if tc == 3 else None, label=f"p2_ntr_{u}" if tc == 3 else None)
                c.wge("vector", "T", f"p2_ntr_{u}")
                for tc in range(4):
                    c.do("vector", lambda e, tc=tc: e.reciprocal(
                        A["rnw"][:, tc:tc + 1], ps["pf"][:, tc * 128:tc * 128 + 1]))
                c.prot(A, f"p2_rnwp_{u}")
                c.do("vector", lambda e: e.tensor_scalar(
                    A["rnw"][:, 0:4], A["rnw"][:, 0:4], A["cols"][:, 0:1], None, ALU.mult),
                    inc="V", label=f"p2_rnw_{u}")
                if u >= 1:
                    c.wge("vector", "dG", f"p2_st_{u-1}")
                if DEBUG2:
                    c.wge("gpsimd", "V", f"p2_rnw_{u}")
                    c.do("gpsimd", lambda e, u=u: e.dma_start(
                        dbg2_nrow[:, u * 512:(u + 1) * 512], A["nrow_sb"][:, :]), inc="dG")
                    c.do("gpsimd", lambda e, u=u: e.dma_start(
                        dbg2_rnw[:, u * 4:(u + 1) * 4], A["rnw"][:, 0:4]), inc="dG")

                for hp in range(4):
                    if hp >= 1:
                        c.wge("tensor", "V", f"p2_amev_{u}_{hp-1}")
                    for fc in range(4):
                        c.do("tensor", lambda e, fc=fc, hp=hp, sc=sc: e.matmul(
                            ps["pe"][:, 0:512], A["smem"][:, fc, hp * 128:(hp + 1) * 128],
                            A["eluq"][:, fc, sc * 512:(sc + 1) * 512],
                            start=(fc == 0), stop=(fc == 3)),
                            inc="T" if fc == 3 else None,
                            label=f"p2_am_{u}_{hp}" if fc == 3 else None)
                    c.wge("vector", "T", f"p2_am_{u}_{hp}")
                    if u > 0 or hp > 0:
                        c.wge("vector", "T", f"p2_amtr_{u}_{hp-1}" if hp > 0
                              else f"p2_amtr_{u-1}_3")
                        if DEBUG2:
                            c.wge("vector", "dG", f"p2_amdmp_{u}_{hp-1}" if hp > 0
                                  else f"p2_amdmp_{u-1}_3")
                    c.do("vector", lambda e: e.tensor_copy(A["am_sb"][:, 0:512], ps["pe"][:, 0:512]))
                    c.do("vector", lambda e: e.tensor_copy(A["cols"][:, 12:13], A["cols"][:, 13:14]),
                         inc="V", label=f"p2_amev_{u}_{hp}")

                    for hd in range(2):
                        h = 2 * hp + hd
                        pq = ps["pa"] if hd == 0 else ps["pb"]
                        ppv = ps["pc"] if hd == 0 else ps["pd"]
                        pt_sb = A["ptA"] if hd == 0 else A["ptB"]
                        plo, phi = (0, 64) if hd == 0 else (64, 128)
                        qt = g * 4 + hp
                        if u > 0 or hp > 0:
                            c.wge("tensor", "V", f"p2_pvev_{u}_{hp-1}_{hd}" if hp > 0
                                  else f"p2_pvev_{u-1}_3_{hd}")
                        for grp in range(8):
                            if grp >= 1:
                                c.wge("tensor", "S", f"p2_exp_{u}_{hp}_{hd}_{grp-1}")
                            for cc2 in range(2):
                                tch = grp * 2 + cc2
                                c.do("tensor", lambda e, tch=tch, cc2=cc2, qt=qt, hp=hp,
                                     plo=plo, phi=phi, pq=pq, sc=sc: e.matmul(
                                    pq[:, cc2 * 512:(cc2 + 1) * 512],
                                    A["k_fm"][plo:phi, hp, tch * 128:(tch + 1) * 128],
                                    A["q_fm"][plo:phi, qt, sc * 512:(sc + 1) * 512],
                                    start=True, stop=True),
                                    inc="T" if cc2 == 1 else None,
                                    label=f"p2_qk_{u}_{hp}_{hd}_{grp}" if cc2 == 1 else None)
                            c.wge("scalar", "T", f"p2_qk_{u}_{hp}_{hd}_{grp}")
                            if u == 0 and hp == 0 and hd == 0 and grp == 0:
                                c.wge("scalar", "V", "p1_bias")
                            half = grp % 2
                            c.do("scalar", lambda e, pq=pq, pt_sb=pt_sb, half=half: e.activation(
                                pt_sb[:, half * 1024:(half + 1) * 1024], pq[:, 0:1024], ACT.Exp,
                                bias=A["cols"][:, 2:3], scale=0.125))
                            c.do("scalar", lambda e: e.copy(A["cols"][:, 12:13], A["cols"][:, 15:16]),
                                inc="S", label=f"p2_exp_{u}_{hp}_{hd}_{grp}")
                            c.wge("tensor", "S", f"p2_exp_{u}_{hp}_{hd}_{grp}")
                            for cc2 in range(2):
                                tch = grp * 2 + cc2
                                c.do("tensor", lambda e, tch=tch, cc2=cc2, h=h, grp=grp,
                                     ppv=ppv, pt_sb=pt_sb, half=half: e.matmul(
                                    ppv[0:65, 0:512], A["v_tm"][:, tch, h * 65:(h + 1) * 65],
                                    pt_sb[:, half * 1024 + cc2 * 512:half * 1024 + (cc2 + 1) * 512],
                                    start=(grp == 0 and cc2 == 0), stop=(grp == 7 and cc2 == 1)),
                                    inc="T" if (grp == 7 and cc2 == 1) else None,
                                    label=f"p2_pv_{u}_{hp}_{hd}" if (grp == 7 and cc2 == 1) else None)
                        c.wge("vector", "T", f"p2_pv_{u}_{hp}_{hd}")
                        c.do("vector", lambda e, hd=hd, ppv=ppv: e.tensor_copy(
                            A["pv_sb"][0:65, hd * 512:(hd + 1) * 512], ppv[0:65, 0:512]))
                        c.do("vector", lambda e: e.tensor_copy(A["cols"][:, 12:13], A["cols"][:, 13:14]),
                            inc="V", label=f"p2_pvev_{u}_{hp}_{hd}")
                    c.wge("tensor", "V", f"p2_pvev_{u}_{hp}_1")
                    c.wge("tensor", "V", f"p2_rnw_{u}" if hp == 0 else f"p2_ambl_{u}_{hp-1}")
                    for hd in range(2):
                        for tc in range(4):
                            c.do("tensor", lambda e, hd=hd, tc=tc: e.transpose(
                                ps["pf"][:, hd * 512 + tc * 128:hd * 512 + tc * 128 + 65],
                                A["pv_sb"][0:65, hd * 512 + tc * 128:hd * 512 + (tc + 1) * 128],
                                IB[0:65, 0:65]),
                                inc="T" if (hd == 1 and tc == 3) else None,
                                label=f"p2_pvtr_{u}_{hp}" if (hd == 1 and tc == 3) else None)
                    c.wge("vector", "T", f"p2_pvtr_{u}_{hp}")
                    for dc in range(8):
                        c.do("vector", lambda e, dc=dc: e.tensor_copy(
                            A["dtmp"][:, dc:dc + 1], ps["pf"][:, dc * 128 + 64:dc * 128 + 65]))
                    c.do("vector", lambda e: e.tensor_copy(A["cols"][:, 12:13], A["cols"][:, 13:14]),
                        inc="V", label=f"p2_dcp_{u}_{hp}")
                    c.wge("scalar", "V", f"p2_dcp_{u}_{hp}")
                    c.do("scalar", lambda e: e.activation(
                        A["rdw"][:, 0:4], A["dtmp"][:, 0:4], ACT.Ln, bias=0.0, scale=1099511627776.0))
                    c.do("scalar", lambda e: e.activation(
                        A["dtmp"][:, 4:8], A["dtmp"][:, 4:8], ACT.Ln, bias=0.0, scale=1099511627776.0))
                    c.ssep(A)
                    c.do("scalar", lambda e: e.activation(
                        A["rdw8"][:, 0:4], A["rdw"][:, 0:4], ACT.Exp, bias=0.0, scale=-1.0))
                    c.do("scalar", lambda e: e.activation(
                        A["rdw8"][:, 4:8], A["dtmp"][:, 4:8], ACT.Exp, bias=0.0, scale=-1.0))
                    c.do("scalar", lambda e: e.copy(A["cols"][:, 12:13], A["cols"][:, 15:16]),
                        inc="S", label=f"p2_rd_{u}_{hp}")
                    c.wge("vector", "S", f"p2_rd_{u}_{hp}")
                    c.do("vector", lambda e: e.tensor_scalar(
                        A["rdw8"][:, 0:8], A["rdw8"][:, 0:8], A["cols"][:, 14:15], None, ALU.mult))
                    c.vsep(A)
                    for hd in range(2):
                        h = 2 * hp + hd
                        for tc in range(4):
                            bcol = hd * 512 + tc * 128
                            c.do("vector", lambda e, bcol=bcol, tc=tc, h=h, hd=hd: e.tensor_scalar(
                                A["stage"][:, tc, h * 64:h * 64 + 64],
                                ps["pf"][:, bcol:bcol + 64],
                                A["rdw8"][:, hd * 4 + tc:hd * 4 + tc + 1], None, ALU.mult),
                                inc="V" if (hd == 1 and tc == 3) else None,
                                label=f"p2_bl_{u}_{hp}" if (hd == 1 and tc == 3) else None)
                    c.wge("tensor", "V", f"p2_bl_{u}_{hp}")
                    c.wge("tensor", "V", f"p2_amev_{u}_{hp}")
                    for tc in range(4):
                        c.do("tensor", lambda e, tc=tc: e.transpose(
                            ps["pf"][:, tc * 128:(tc + 1) * 128],
                            A["am_sb"][:, tc * 128:(tc + 1) * 128], IB[:]),
                            inc="T" if tc == 3 else None,
                            label=f"p2_amtr_{u}_{hp}" if tc == 3 else None)
                    c.wge("vector", "T", f"p2_amtr_{u}_{hp}")
                    for tc in range(4):
                        c.do("vector", lambda e, tc=tc, hp=hp: e.scalar_tensor_tensor(
                            A["stage"][:, tc, hp * 128:(hp + 1) * 128],
                            ps["pf"][:, tc * 128:(tc + 1) * 128],
                            A["rnw"][:, tc:tc + 1],
                            A["stage"][:, tc, hp * 128:(hp + 1) * 128],
                            ALU.mult, ALU.add))
                    c.do("vector", lambda e: e.tensor_copy(A["cols"][:, 12:13], A["cols"][:, 13:14]),
                        inc="V", label=f"p2_ambl_{u}_{hp}")
                    if DEBUG2:
                        uh = u * 4 + hp
                        c.wge("gpsimd", "V", f"p2_amev_{u}_{hp}")
                        c.do("gpsimd", lambda e, uh=uh: e.dma_start(
                            dbg2_am[:, uh * 512:(uh + 1) * 512], A["am_sb"][:, 0:512]), inc="dG",
                            label=f"p2_amdmp_{u}_{hp}")
                        c.wge("gpsimd", "V", f"p2_pvev_{u}_{hp}_1")
                        c.do("gpsimd", lambda e, uh=uh: e.dma_start(
                            dbg2_pv[:, uh * 1024:(uh + 1) * 1024], A["pv_sb"][0:65, :]), inc="dG")
                        c.wge("gpsimd", "V", f"p2_bl_{u}_{hp}")
                        c.do("gpsimd", lambda e, uh=uh: e.dma_start(
                            dbg2_rd[:, uh * 8:(uh + 1) * 8], A["rdw8"][:, 0:8]), inc="dG")
                if DEBUG2 and sc == 1:
                    c.wge("gpsimd", "T", f"p2_am_{u}_3")
                    c.do("gpsimd", lambda e, g=g: e.dma_start(
                        dbg2_el2[:, g * 4096:(g + 1) * 4096],
                        A["eluq"][:, :, :].rearrange("p a b -> p (a b)")), inc="dG",
                        label=f"p2_eldmp_{g}")
                c.wge("gpsimd", "V", f"p2_ambl_{u}_3")
                c.do("gpsimd", lambda e, g=g, sc=sc: e.dma_start(
                    bass.AP(attn_scr, (sc * 512) * D + g * 512,
                            [[D, 128], [128 * D, 4], [1, 512]]),
                    A["stage"][:, :, :]), inc="dG", label=f"p2_st_{u}")

            if not SCALE_INLINE:
                for cn in range(128):
                    scale_chunk(cn)
            c.wge("vector", "V", "p2sc_red_127")
            c.prot(A, "p2_scpre")
            c.do("vector", lambda e: e.reduce_sum(A["sc_cfc"][:, 0:1], A["sc_cfc"][:, :], axis=AX.X))
            c.do("vector", lambda e: e.reduce_sum(A["sc_cp"][:, 0:1], A["sc_cp"][:, :], axis=AX.X))
            c.prot(A, "p2_scred")
            c.wge("tensor", "V", "p2_scred")
            c.wge("tensor", "V", "p2_amev_7_3")
            c.do("tensor", lambda e: e.matmul(ps["pe"][0:1, 3:4], A["onescol"][:, :],
                                              A["sc_cfc"][:, 0:1], start=True, stop=True))
            c.do("tensor", lambda e: e.matmul(ps["pe"][0:1, 4:5], A["onescol"][:, :],
                                              A["sc_cp"][:, 0:1], start=True, stop=True),
                 inc="T", label="p2_scmm")
            c.wge("vector", "T", "p2_scmm")
            c.do("vector", lambda e: e.tensor_scalar(A["sc1"][0:1, 0:1], ps["pe"][0:1, 3:4],
                                                     1.0 / (D * H), EPS_Q, ALU.mult, ALU.max))
            c.do("vector", lambda e: e.tensor_scalar(A["sc1"][0:1, 1:2], ps["pe"][0:1, 4:5],
                                                     1.0 / (D * H), EPS_Q, ALU.mult, ALU.max),
                 inc="V", label="p2_s12")
            c.wge("tensor", "V", "p2_s12")
            c.do("tensor", lambda e: e.matmul(ps["pe"][:, 5:6], A["onesrow"][:, :],
                                              A["sc1"][0:1, 0:1], start=True, stop=True))
            c.do("tensor", lambda e: e.matmul(ps["pe"][:, 6:7], A["onesrow"][:, :],
                                              A["sc1"][0:1, 1:2], start=True, stop=True),
                 inc="T", label="p2_sbc")
            c.wge("vector", "T", "p2_sbc")
            c.do("vector", lambda e: e.tensor_copy(A["cols"][:, 10:11], ps["pe"][:, 5:6]))
            c.do("vector", lambda e: e.tensor_copy(A["cols"][:, 11:12], ps["pe"][:, 6:7]))
            c.do("vector", lambda e: e.tensor_scalar(A["cols"][:, 3:4], A["cols"][:, 10:11],
                                                     0.5, None, ALU.mult))
            c.do("vector", lambda e: e.tensor_scalar(A["cols"][:, 4:5], A["cols"][:, 10:11],
                                                     -0.5, None, ALU.mult))
            c.do("vector", lambda e: e.tensor_scalar(A["cols"][:, 5:6], A["cols"][:, 11:12],
                                                     0.5, None, ALU.mult))
            c.do("vector", lambda e: e.tensor_scalar(A["cols"][:, 6:7], A["cols"][:, 11:12],
                                                     -0.5, None, ALU.mult))
            c.do("vector", lambda e: e.tensor_mul(A["cols"][:, 7:8], A["cols"][:, 10:11],
                                                  A["cols"][:, 10:11]))
            c.vsep(A)
            c.do("vector", lambda e: e.tensor_scalar(A["cols"][:, 8:9], A["cols"][:, 7:8],
                                                     NSTEPS, None, ALU.mult))
            c.do("vector", lambda e: e.tensor_scalar(A["cols"][:, 9:10], A["cols"][:, 11:12],
                                                     1.0 / NSTEPS, None, ALU.mult),
                 inc="V", label="p2_cols")

        def ph_p3(c: Ctx):
            for j in range(64):
                buf = j % 2
                if j < 2:
                    c.wge("sync", "V", "p2sc_red_127")
                else:
                    c.wge("sync", "V", f"p3_q1_{j-2}")
                c.do("sync", lambda e, j=j, buf=buf: e.dma_start(
                    A["cfc_buf"][:, buf, :, :],
                    bass.AP(cfc, j * 128, [[H, 128], [128 * H, 16], [1, 128]])),
                    inc="dL", label=f"p3_ld_{j}")
                c.wge("vector", "dL", f"p3_ld_{j}")
                if j == 0:
                    c.wge("vector", "V", "p2_cols")
                if j >= 2:
                    c.wge("vector", "T", f"p3_mm_{j-2}_1")
                c.do("vector", lambda e, buf=buf: e.tensor_scalar(
                    A["q1_buf"][:, buf, :, :], A["cfc_buf"][:, buf, :, :],
                    A["cols"][:, 4:5], None, ALU.is_le))
                c.vsep(A)
                c.do("vector", lambda e, buf=buf: e.scalar_tensor_tensor(
                    A["q1_buf"][:, buf, :, :], A["cfc_buf"][:, buf, :, :],
                    A["cols"][:, 3:4], A["q1_buf"][:, buf, :, :],
                    ALU.is_ge, ALU.subtract), inc="V", label=f"p3_q1_{j}")
                c.wge("tensor", "V", f"p3_q1_{j}")
                if j >= 1:
                    c.wge("tensor", "S", f"p3_ev_{j-1}_1")
                for tch in range(2):
                    pdst = ps["pa"] if tch == 0 else ps["pb"]
                    for d in range(16):
                        c.do("tensor", lambda e, d=d, tch=tch, buf=buf, pdst=pdst: e.matmul(
                            pdst[:, 0:512], A["q1_buf"][:, buf, d, :],
                            A["xn_own"][:, d, tch * 512:(tch + 1) * 512],
                            start=(d == 0), stop=(d == 15)),
                            inc="T" if d == 15 else None,
                            label=f"p3_mm_{j}_{tch}" if d == 15 else None)
                for tch in range(2):
                    pdst = ps["pa"] if tch == 0 else ps["pb"]
                    c.wge("scalar", "T", f"p3_mm_{j}_{tch}")
                    c.do("scalar", lambda e, j=j, tch=tch, pdst=pdst: e.activation(
                        A["h_st"][:, j, tch * 512:(tch + 1) * 512], pdst[:, 0:512], ACT.Relu),
                        inc="S", label=f"p3_ev_{j}_{tch}")
                c.wge("vector", "S", f"p3_ev_{j}_1")
                c.do("vector", lambda e, j=j: e.tensor_max(
                    A["hmax"][:, 0:1024], A["hmax"][:, 0:1024], A["h_st"][:, j, :]),
                    inc="V", label=f"p3_hm_{j}")

            c.wge("tensor", "V", "p3_hm_63")
            c.wge("tensor", "V", "p2_ambl_7_3")
            for tb in range(8):
                c.do("tensor", lambda e, tb=tb: e.transpose(
                    ps["pf"][:, (tb % 8) * 128:(tb % 8) * 128 + 128],
                    A["hmax"][:, tb * 128:(tb + 1) * 128], IB[:]),
                    inc="T" if tb == 7 else None, label="p3_hmtr" if tb == 7 else None)
            c.wge("vector", "T", "p3_hmtr")
            for tb in range(8):
                c.do("vector", lambda e, tb=tb: e.reduce_max(
                    A["rmax_tok"][:, tb:tb + 1], ps["pf"][:, tb * 128:(tb + 1) * 128], axis=AX.X))
            c.prot(A, "p3_redp")
            c.do("vector", lambda e: e.tensor_mul(A["oscale"][:, 0:8], A["rmax_tok"][:, 0:8],
                                                  A["rmax_tok"][:, 0:8]))
            c.vsep(A)
            c.do("vector", lambda e: e.tensor_scalar(A["oscale"][:, 0:8], A["oscale"][:, 0:8],
                                                     A["cols"][:, 7:8], EPS_H, ALU.mult, ALU.max))
            c.vsep(A)
            c.do("vector", lambda e: e.reciprocal(A["rmax_tok"][:, 0:8], A["oscale"][:, 0:8]))
            c.prot(A, "p3_rcpp")
            c.do("vector", lambda e: e.tensor_scalar(A["rmax_tok"][:, 0:8], A["rmax_tok"][:, 0:8],
                                                     A["cols"][:, 8:9], None, ALU.mult))
            c.do("vector", lambda e: e.tensor_scalar(A["oscale"][:, 0:8], A["oscale"][:, 0:8],
                                                     A["cols"][:, 9:10], None, ALU.mult),
                 inc="V", label="p3_mtok")
            for tc in range(8):
                c.do("vector", lambda e, tc=tc: e.tensor_scalar(
                    A["ln_junk"][:, tc * 128:(tc + 1) * 128], A["onesT"][:, :],
                    A["rmax_tok"][:, tc:tc + 1], None, ALU.mult),
                    inc="V" if tc == 7 else None, label="p3_mrep" if tc == 7 else None)
            c.wge("tensor", "V", "p3_mrep")
            c.wge("tensor", "S", "p3_ev_63_1")
            for tc in range(8):
                c.do("tensor", lambda e, tc=tc: e.transpose(
                    ps["pa"][:, (tc % 2) * 512 + (tc // 2) * 128:(tc % 2) * 512 + (tc // 2) * 128 + 128]
                    if False else ps["pa"][:, tc * 128:tc * 128 + 128],
                    A["ln_junk"][:, tc * 128:(tc + 1) * 128], IF[:]),
                    inc="T" if tc == 7 else None, label="p3_mbc" if tc == 7 else None)
            c.wge("vector", "T", "p3_mbc")
            c.do("vector", lambda e: e.tensor_copy(A["m_bcast"][:, 0:1024], ps["pa"][:, 0:1024]),
                 inc="V", label="p3_mb")

            for j in range(64):
                if j == 0:
                    c.wge("scalar", "V", "p3_mb")
                else:
                    c.wge("scalar", "V", f"p3_rq_{j-1}")
                c.do("scalar", lambda e, j=j: e.activation(
                    A["sqh"][:, 0:1024], A["h_st"][:, j, :], ACT.Square),
                    inc="S", label=f"p3_sq_{j}")
                c.wge("vector", "S", f"p3_sq_{j}")
                c.do("vector", lambda e: e.tensor_mul(
                    A["i32h"][:, 0:1024], A["sqh"][:, 0:1024], A["m_bcast"][:, 0:1024]))
                c.vsep(A)
                c.do("vector", lambda e, j=j: e.tensor_scalar(
                    A["h_st"][:, j, :], A["i32h"][:, 0:1024], 0.0, None, ALU.add),
                    inc="V", label=f"p3_rq_{j}")

        def ph_p4a(c: Ctx):
            for ic in range(16):
                for s in range(16):
                    buf = s % 2
                    if ic == 0 and s < 2:
                        c.wge("sync", "V", "p3_q1_63")
                    else:
                        c.wge("sync", "V", f"p4_q2a_{ic}_{s-2}" if s >= 2
                              else f"p4_q2a_{ic-1}_{14 + s}")
                    c.do("sync", lambda e, ic=ic, s=s, buf=buf: e.dma_start(
                        A["cp_buf"][:, buf, :],
                        cproj[ic * 128:(ic + 1) * 128, s * 512:(s + 1) * 512]),
                        inc="dL", label=f"p4_ld_{ic}_{s}")
                    c.wge("vector", "dL", f"p4_ld_{ic}_{s}")
                    if ic == 0 and s == 0:
                        c.wge("vector", "V", "p3_rq_63")
                    if s == 0 and ic > 0:
                        c.wge("vector", "T", f"p4_mm_{ic-1}_1")
                    c.do("vector", lambda e, buf=buf: e.tensor_scalar(
                        A["q2_sl"][:, 0:512], A["cp_buf"][:, buf, :],
                        A["cols"][:, 6:7], None, ALU.is_le))
                    c.vsep(A)
                    c.do("vector", lambda e, buf=buf: e.scalar_tensor_tensor(
                        A["q2_sl"][:, 0:512], A["cp_buf"][:, buf, :],
                        A["cols"][:, 5:6], A["q2_sl"][:, 0:512],
                        ALU.is_ge, ALU.subtract), inc="V", label=f"p4_q2a_{ic}_{s}")
                    c.wge("tensor", "V", f"p4_q2a_{ic}_{s}")
                    if ic > 0 or s > 0:
                        c.wge("tensor", "V", f"p4_q2_{ic}_{s-1}" if s > 0
                              else f"p4_q2_{ic-1}_15")
                        if s == 0:
                            c.wge("tensor", "V", f"p4_osc_{ic-1}_1")
                    elif True:
                        c.wge("tensor", "V", "p3_mb")
                    for tt in range(4):
                        c.do("tensor", lambda e, tt=tt: e.transpose(
                            ps["pf"][:, tt * 128:(tt + 1) * 128],
                            A["q2_sl"][:, tt * 128:(tt + 1) * 128], IB[:]),
                            inc="T" if tt == 3 else None,
                            label=f"p4_wtr_{ic}_{s}" if tt == 3 else None)
                    c.wge("vector", "T", f"p4_wtr_{ic}_{s}")
                    c.do("vector", lambda e, s=s: e.tensor_copy(
                        A["w2T"][:, 4 * s:4 * s + 4, :], ps["pf"][:, 0:512]),
                        inc="V", label=f"p4_q2_{ic}_{s}")
                c.wge("tensor", "V", f"p4_q2_{ic}_15")
                for tch in range(2):
                    pdst = ps["pc"] if tch == 0 else ps["pd"]
                    if ic > 0:
                        c.wge("tensor", "V", f"p4_oev_{ic-1}_{tch}")
                    for j in range(64):
                        c.do("tensor", lambda e, j=j, tch=tch, pdst=pdst: e.matmul(
                            pdst[:, 0:512], A["w2T"][:, j, :],
                            A["h_st"][:, j, tch * 512:(tch + 1) * 512],
                            start=(j == 0), stop=(j == 63)),
                            inc="T" if j == 63 else None,
                            label=f"p4_mm_{ic}_{tch}" if j == 63 else None)
                    c.wge("vector", "T", f"p4_mm_{ic}_{tch}")
                    if ic > 0 or tch > 0:
                        c.wge("vector", "T", f"p4_otr_{ic}_{tch-1}" if tch > 0
                              else f"p4_otr_{ic-1}_1")
                    c.do("vector", lambda e, pdst=pdst: e.tensor_copy(
                        A["o_sb"][:, 0:512], pdst[:, 0:512]), inc="V",
                        label=f"p4_oev_{ic}_{tch}")
                    c.wge("tensor", "V", f"p4_oev_{ic}_{tch}")
                    if ic > 0 or tch > 0:
                        c.wge("tensor", "V", f"p4_osc_{ic}_{tch-1}" if tch > 0
                              else f"p4_osc_{ic-1}_1")
                    for tt in range(4):
                        c.do("tensor", lambda e, tt=tt: e.transpose(
                            ps["pf"][:, 512 + tt * 128:512 + (tt + 1) * 128],
                            A["o_sb"][:, tt * 128:(tt + 1) * 128], IB[:]),
                            inc="T" if tt == 3 else None,
                            label=f"p4_otr_{ic}_{tch}" if tt == 3 else None)
                    c.wge("vector", "T", f"p4_otr_{ic}_{tch}")
                    if ic > 0:
                        c.wge("vector", "dG", f"p4_st_{ic-1}_{tch}")
                    for tt in range(4):
                        tb = tch * 4 + tt
                        c.do("vector", lambda e, tt=tt, tb=tb, tch=tch: e.tensor_scalar(
                            A["ffn_sb"][:, tch, tt * 128:(tt + 1) * 128],
                            ps["pf"][:, 512 + tt * 128:512 + (tt + 1) * 128],
                            A["oscale"][:, tb:tb + 1], None, ALU.mult),
                            inc="V" if tt == 3 else None,
                            label=f"p4_osc_{ic}_{tch}" if tt == 3 else None)
                    c.wge("gpsimd", "V", f"p4_osc_{ic}_{tch}")
                    c.do("gpsimd", lambda e, ic=ic, tch=tch: e.dma_start(
                        bass.AP(ffn_scr, (tch * 512) * D + ic * 128,
                                [[D, 128], [128 * D, 4], [1, 128]]),
                        A["ffn_sb"][:, tch, :]), inc="dG", label=f"p4_st_{ic}_{tch}")

        def ph_p4b(c: Ctx):
            for tb in range(8):
                buf = tb % 2
                if tb >= 2:
                    c.wge("gpsimd", "V", f"p4b_add_{tb-2}")
                if tb == 0:
                    c.wge("gpsimd", "dG", "p4_st_15_1")
                    c.wge("gpsimd", "T", "p4_mm_15_1")
                c.do("gpsimd", lambda e, tb=tb, buf=buf: e.dma_start(
                    A["cmb_x"][:, buf, :], x_own[tb * 128:(tb + 1) * 128, :]), inc="dG",
                    label=f"p4b_lx_{tb}")
                c.do("gpsimd", lambda e, tb=tb, buf=buf: e.dma_start(
                    A["cmb_a"][:, buf, :], attn_scr[tb * 128:(tb + 1) * 128, :]), inc="dG",
                    label=f"p4b_la_{tb}")
                c.do("gpsimd", lambda e, tb=tb, buf=buf: e.dma_start(
                    A["cmb_f"][:, buf, :], ffn_scr[tb * 128:(tb + 1) * 128, :]), inc="dG",
                    label=f"p4b_lf_{tb}")
                c.wge("vector", "dG", f"p4b_lf_{tb}")
                c.do("vector", lambda e, buf=buf: e.tensor_add(
                    A["cmb_o"][:, buf, :], A["cmb_x"][:, buf, :], A["cmb_a"][:, buf, :]))
                c.vsep(A)
                c.do("vector", lambda e, buf=buf: e.tensor_add(
                    A["cmb_o"][:, buf, :], A["cmb_o"][:, buf, :], A["cmb_f"][:, buf, :]),
                    inc="V", label=f"p4b_add_{tb}")
                c.wge("gpsimd", "V", f"p4b_add_{tb}")
                c.do("gpsimd", lambda e, tb=tb, buf=buf: e.dma_start(
                    out_d[tb * 128:(tb + 1) * 128, :], A["cmb_o"][:, buf, :]), inc="dG",
                    label=f"p4b_st_{tb}")
            c.wge("gpsimd", "dG", "p4b_st_7")
            c.wge("gpsimd", "dL", "p4_ld_15_15")

        def ph_dbg(c: Ctx):
            if not DEBUG:
                return
            c.wge("gpsimd", "V", "p1_bias")
            c.do("gpsimd", lambda e: e.dma_start(
                dbg_q[:, :], A["q_fm"][:, :, :].rearrange("p a b -> p (a b)")), inc="dG")
            c.do("gpsimd", lambda e: e.dma_start(
                dbg_k[:, :], A["k_fm"][:, :, :].rearrange("p a b -> p (a b)")), inc="dG")
            c.do("gpsimd", lambda e: e.dma_start(
                dbg_v[:, :], A["v_tm"][:, :, :].rearrange("p a b -> p (a b)")), inc="dG")
            c.do("gpsimd", lambda e: e.dma_start(dbg_c[:, :], A["cols"][:, :]), inc="dG")
            if DEBUG2:
                c.do("gpsimd", lambda e: e.dma_start(
                    dbg2_sm1[:, :], A["smem"][:, :, :].rearrange("p a b -> p (a b)")), inc="dG")
            c.do("gpsimd", lambda e: e.dma_start(dbg_s[:, :], A["sc1"][:, :]), inc="dG")
            c.do("gpsimd", lambda e: e.dma_start(dbg_h[:, :], A["shist"][:, :]), inc="dG")
            c.do("gpsimd", lambda e: e.dma_start(
                dbg_xn[:, :], A["xn_own"][:, :, :].rearrange("p a b -> p (a b)")), inc="dG",
                 label="dbg_done")
            c.wge("gpsimd", "dG", "dbg_done")

        def ph_dbg2(c: Ctx):
            if not DEBUG:
                return
            c.wge("gpsimd", "V", "p2_el_3_3")
            c.wge("gpsimd", "V", "p2_ambl_7_3")
            c.do("gpsimd", lambda e: e.dma_start(
                dbg_e[:, :], A["eluq"][:, :, :].rearrange("p a b -> p (a b)")), inc="dG")
            c.do("gpsimd", lambda e: e.dma_start(dbg_pv[0:65, :], A["pv_sb"][0:65, :]), inc="dG")
            c.do("gpsimd", lambda e: e.dma_start(dbg_rd[:, :], A["rdw8"][:, :]), inc="dG")
            if DEBUG2:
                c.do("gpsimd", lambda e: e.dma_start(
                    dbg2_sm2[:, :], A["smem"][:, :, :].rearrange("p a b -> p (a b)")), inc="dG")
                c.do("gpsimd", lambda e: e.dma_start(
                    dbg2_smr[:, :], A["smemn_rep"][:, :, :].rearrange("p a b -> p (a b)")), inc="dG")
            c.do("gpsimd", lambda e: e.dma_start(dbg_dt[:, :], A["dtmp"][:, :]), inc="dG")
            c.do("gpsimd", lambda e: e.dma_start(dbg_pt[:, :], A["ptB"][:, :]), inc="dG",
                label="dbg2_done")
            c.wge("gpsimd", "dG", "dbg2_done")

        phases = [ph_setup, ph_p0, ph_p1, ph_dbg, ph_p2, ph_dbg2, ph_p3, ph_p4a, ph_p4b]
        if PHASES_N is not None:
            phases = phases[:PHASES_N]

        plan = Ctx("plan", marks=marks, counts=counts)
        for ph in phases:
            ph(plan)

        for ph in phases:
            with nc.Block() as blk:
                @blk.sync
                def _(e, ph=ph):
                    ph(Ctx("sync", e, sems, marks))

                @blk.gpsimd
                def _(e, ph=ph):
                    ph(Ctx("gpsimd", e, sems, marks))

                @blk.tensor
                def _(e, ph=ph):
                    ph(Ctx("tensor", e, sems, marks))

                @blk.vector
                def _(e, ph=ph):
                    ph(Ctx("vector", e, sems, marks))

                @blk.scalar
                def _(e, ph=ph):
                    ph(Ctx("scalar", e, sems, marks))

    return nc


_NC_CACHE = {}


def kernel(L0_attn_x, L0_attn_Wq, L0_attn_Wk, L0_attn_Wv, L0_attn_memory,
           L0_attn_memory_norm, L0_attn_memory_weight, L0_ffn_c_fc, L0_ffn_c_proj):
    x = np.ascontiguousarray(np.asarray(L0_attn_x, dtype=np.float32))
    wq_np = np.ascontiguousarray(np.asarray(L0_attn_Wq, dtype=np.float32))
    wk_np = np.ascontiguousarray(np.asarray(L0_attn_Wk, dtype=np.float32))
    wv_np = np.ascontiguousarray(np.asarray(L0_attn_Wv, dtype=np.float32))
    mem_np = np.ascontiguousarray(np.asarray(L0_attn_memory, dtype=np.float32))
    memn_np = np.ascontiguousarray(np.asarray(L0_attn_memory_norm, dtype=np.float32))
    memw_np = np.asarray(L0_attn_memory_weight, dtype=np.float32).reshape(1, 1)
    cfc_np = np.ascontiguousarray(np.asarray(L0_ffn_c_fc, dtype=np.float32))
    cproj_np = np.ascontiguousarray(np.asarray(L0_ffn_c_proj, dtype=np.float32))

    if "nc" not in _NC_CACHE:
        _NC_CACHE["nc"] = build_program()
    nc = _NC_CACHE["nc"]

    in_maps = []
    for core in range(8):
        b, h = core // 2, core % 2
        in_maps.append({
            "x_own": np.ascontiguousarray(x[b, h * 1024:(h + 1) * 1024, :]),
            "x_par": np.ascontiguousarray(x[b, (1 - h) * 1024:(2 - h) * 1024, :]),
            "wq": wq_np, "wk": wk_np, "wv": wv_np, "memd": mem_np, "memn": memn_np,
            "memw": memw_np, "cfc": cfc_np, "cproj": cproj_np,
        })
    res = run_bass_kernel_spmd(nc, in_maps, core_ids=list(range(8)))
    out = np.empty((4, 2048, 2048), dtype=np.float32)
    for core in range(8):
        b, h = core // 2, core % 2
        out[b, h * 1024:(h + 1) * 1024, :] = np.asarray(res.results[core]["out"])
    return out


# version 03
# speedup vs baseline: 1.1475x; 1.1475x over previous
import contextlib

import numpy as np
import ml_dtypes
import concourse.bass as bass
import concourse.mybir as mybir
from concourse.bass_utils import run_bass_kernel_spmd

F32 = mybir.dt.float32
BF16 = mybir.dt.bfloat16
I32 = mybir.dt.int32
ALU = mybir.AluOpType
ACT = mybir.ActivationFunctionType
AX = mybir.AxisListType

D = 2048
H = 8192
T_OWN = 1024
EPS_LN = 1e-5
EPS_Q = 1e-5
EPS_H = 1e-5
NSTEPS = 255.0
DEBUG = False
DEBUG2 = False
PHASES_N = None
P1_MODE = 0
P1_QIC = 16
P1_SUB = 99
BISECT = False
SCALE_INLINE = True
SQ_SBUF = True


class Ctx:

    def __init__(self, mode, eng=None, sems=None, marks=None, counts=None):
        self.mode = mode
        self.e = eng
        self.sems = sems
        self.marks = marks
        self.counts = counts

    def val(self, sem, label):
        return self.marks[label][sem]

    def on(self, eng):
        return self.mode == eng

    def wge(self, eng, sem, label):
        if label not in self.marks and BISECT:
            return
        if self.mode == eng:
            self.e.wait_ge(self.sems[sem], self.marks[label][sem])

    def vsep(self, A):
        self.do("vector", lambda e: e.tensor_copy(A["cols"][:, 12:13], A["cols"][:, 13:14]))

    def ssep(self, A):
        self.do("scalar", lambda e: e.copy(A["cols"][:, 12:13], A["cols"][:, 13:14]))

    def prot(self, A, label):
        self.do("vector", lambda e: e.tensor_copy(A["cols"][:, 12:13], A["cols"][:, 13:14]),
                inc="V", label=label)
        self.wge("vector", "V", label)

    def sprot(self, A, label):
        self.do("scalar", lambda e: e.copy(A["cols"][:, 12:13], A["cols"][:, 13:14]),
                inc="S", label=label)
        self.wge("scalar", "S", label)

    def do(self, eng, fn, inc=None, label=None):
        k = 16 if inc in ("dL", "dG") else 1
        if self.mode == "plan":
            if inc is not None:
                self.counts[inc] = self.counts.get(inc, 0) + k
            if label is not None:
                assert label not in self.marks, f"dup label {label}"
                self.marks[label] = dict(self.counts)
            return
        if self.mode == eng:
            ins = fn(self.e)
            if inc is not None:
                ins.then_inc(self.sems[inc], k)


def build_program():
    nc = bass.Bass("TRN2", target_bir_lowering=False, detect_race_conditions=False)

    x_own = nc.dram_tensor("x_own", [T_OWN, D], F32, kind="ExternalInput")
    x_par = nc.dram_tensor("x_par", [T_OWN, D], F32, kind="ExternalInput")
    wq = nc.dram_tensor("wq", [D, D], F32, kind="ExternalInput")
    wk = nc.dram_tensor("wk", [512, D], F32, kind="ExternalInput")
    wv = nc.dram_tensor("wv", [512, D], F32, kind="ExternalInput")
    memd = nc.dram_tensor("memd", [512, 512], F32, kind="ExternalInput")
    memn = nc.dram_tensor("memn", [512, 1], F32, kind="ExternalInput")
    memw = nc.dram_tensor("memw", [1, 1], F32, kind="ExternalInput")
    cfc = nc.dram_tensor("cfc", [D, H], F32, kind="ExternalInput")
    cproj = nc.dram_tensor("cproj", [D, H], F32, kind="ExternalInput")
    out_d = nc.dram_tensor("out", [T_OWN, D], F32, kind="ExternalOutput")
    scr_kind = "ExternalOutput" if DEBUG else "Internal"
    attn_scr = nc.dram_tensor("attn_scr", [T_OWN, D], F32, kind=scr_kind)
    ffn_scr = nc.dram_tensor("ffn_scr", [T_OWN, D], F32, kind=scr_kind)
    if DEBUG:
        dbg_q = nc.dram_tensor("dbg_q", [128, 16 * 1024], BF16, kind="ExternalOutput")
        dbg_k = nc.dram_tensor("dbg_k", [128, 4 * 2048], BF16, kind="ExternalOutput")
        dbg_v = nc.dram_tensor("dbg_v", [128, 16 * 520], BF16, kind="ExternalOutput")
        dbg_c = nc.dram_tensor("dbg_c", [128, 16], F32, kind="ExternalOutput")
        dbg_s = nc.dram_tensor("dbg_s", [1, 8], F32, kind="ExternalOutput")
        dbg_e = nc.dram_tensor("dbg_e", [128, 4 * 1024], BF16, kind="ExternalOutput")
        dbg_h = nc.dram_tensor("dbg_h", [128, 128], F32, kind="ExternalOutput")
        dbg_xn = nc.dram_tensor("dbg_xn", [128, 16 * 1024], BF16, kind="ExternalOutput")
        dbg_pv = nc.dram_tensor("dbg_pv", [128, 1024], BF16, kind="ExternalOutput")
        dbg_rd = nc.dram_tensor("dbg_rd", [128, 8], F32, kind="ExternalOutput")
        dbg_dt = nc.dram_tensor("dbg_dt", [128, 8], F32, kind="ExternalOutput")
        dbg_pt = nc.dram_tensor("dbg_pt", [128, 2048], BF16, kind="ExternalOutput")
    if DEBUG2:
        dbg2_pv = nc.dram_tensor("dbg2_pv", [65, 32 * 1024], BF16, kind="ExternalOutput")
        dbg2_am = nc.dram_tensor("dbg2_am", [128, 32 * 512], BF16, kind="ExternalOutput")
        dbg2_rd = nc.dram_tensor("dbg2_rd", [128, 32 * 8], F32, kind="ExternalOutput")
        dbg2_rnw = nc.dram_tensor("dbg2_rnw", [128, 8 * 4], F32, kind="ExternalOutput")
        dbg2_nrow = nc.dram_tensor("dbg2_nrow", [128, 8 * 512], BF16, kind="ExternalOutput")
        dbg2_el = nc.dram_tensor("dbg2_el", [128, 4 * 4096], BF16, kind="ExternalOutput")
        dbg2_el2 = nc.dram_tensor("dbg2_el2", [128, 4 * 4096], BF16, kind="ExternalOutput")
        dbg2_sm1 = nc.dram_tensor("dbg2_sm1", [128, 2048], BF16, kind="ExternalOutput")
        dbg2_sm2 = nc.dram_tensor("dbg2_sm2", [128, 2048], BF16, kind="ExternalOutput")
        dbg2_smr = nc.dram_tensor("dbg2_smr", [128, 512], BF16, kind="ExternalOutput")

    ident_bf_d = nc.inline_tensor(np.eye(128, dtype=np.float32).astype(ml_dtypes.bfloat16), name="ib_d")
    ident_f32_d = nc.inline_tensor(np.eye(128, dtype=np.float32), name="if_d")
    onesrow_d = nc.inline_tensor(np.ones((1, 128), dtype=np.float32), name="or_d")

    A = {}
    SB_BASE = 16544

    def at(name, off, shape, dtype):
        A[name] = nc.alloc_sbuf_tensor_at(name, shape, dtype, offset=SB_BASE + off)

    at("ident_bf", 0, [128, 128], BF16)
    at("ident_f32", 256, [128, 128], F32)
    at("onesrow", 768, [1, 128], F32)
    at("onescol", 1280, [128, 1], F32)
    at("ones2", 1312, [128, 2], BF16)
    at("smem", 1344, [128, 4, 512], BF16)
    at("smemn", 5440, [128, 4, 1], BF16)
    at("cols", 5472, [128, 16], F32)
    at("sc1", 5536, [1, 8], F32)
    at("nq_accf", 5568, [1, 32], F32)
    at("nk_accf", 5696, [1, 16], F32)
    at("sc_cfc", 5760, [128, 64], F32)
    at("sc_cp", 6016, [128, 64], F32)
    at("m_row", 6272, [1, 1024], F32)
    at("m_bcast", 10368, [128, 1024], F32)
    at("rmax_tok", 14464, [128, 8], F32)
    at("oscale", 14496, [128, 8], F32)
    at("stats", 14528, [128, 8], F32)
    at("msb8", 14560, [8, 128], F32)
    at("rnw", 15072, [128, 4], F32)
    at("rdw", 15104, [128, 4], F32)
    at("dtmp", 15136, [128, 8], F32)
    at("rdw8", 15168, [128, 8], F32)
    at("shist", 15232, [128, 128], F32)
    at("memnrow", 15744, [1, 512], F32)
    at("smemn_rep", 17792, [128, 4, 128], BF16)
    at("onesT", 18816, [128, 128], F32)

    at("xn_own", 20480, [128, 16, 1024], BF16)

    at("xn_par", 53248, [128, 16, 1024], BF16)
    at("q_fm", 86016, [128, 16, 1024], BF16)
    at("k_fm", 118784, [128, 4, 2048], BF16)
    at("v_tm", 135168, [128, 16, 520], BF16)
    at("ptA", 151808, [128, 2048], BF16)
    at("ptB", 155904, [128, 2048], BF16)
    at("eluq", 160000, [128, 4, 1024], BF16)
    at("v_sb", 168192, [128, 512], BF16)
    at("stage", 168704, [128, 4, 512], F32)
    at("pv_sb", 176896, [128, 1024], BF16)
    at("am_sb", 178944, [128, 512], BF16)
    at("nrow_sb", 179968, [128, 512], BF16)
    at("sq_sb", 180992, [128, 1024], BF16)
    at("h_st", 53248, [128, 64, 1024], BF16)
    at("cmb_x", 53248, [128, 2, 2048], F32)
    at("cmb_a", 86016, [128, 2, 2048], F32)
    at("cmb_f", 118784, [128, 2, 2048], F32)
    at("cmb_o", 151808, [128, 2, 2048], F32)

    at("x_tile", 184320, [128, 2, 2048], F32)
    at("ln_junk", 200704, [128, 2048], F32)
    at("w_buf", 184320, [128, 2, 2048], F32)
    at("wT_buf", 200704, [128, 16, 128], BF16)
    at("scale_buf", 184320, [128, 2, 2048], F32)
    at("cfc_buf", 184320, [128, 2, 16, 128], F32)
    at("q1_buf", 200704, [128, 2, 16, 128], BF16)
    at("hmax", 208896, [128, 1024], BF16)
    at("sqh", 200704, [128, 1024], F32)
    at("i32h", 204800, [128, 1024], I32)
    at("cp_buf", 184320, [128, 2, 512], F32)
    at("q2_sl", 188416, [128, 512], BF16)
    at("w2T", 189440, [128, 64, 128], BF16)
    at("ffn_sb", 205824, [128, 2, 512], F32)
    at("o_sb", 209920, [128, 512], BF16)

    es = contextlib.ExitStack()
    with es:
        sems = {n: es.enter_context(nc.semaphore(n)) for n in ("dL", "dG", "T", "V", "S")}
        marks, counts = {}, {}
        ps = {}
        for name, shape, dt in (("pa", [128, 1024], F32), ("pb", [128, 1024], F32),
                                ("pc", [128, 512], F32), ("pd", [128, 512], F32),
                                ("pe", [128, 512], F32), ("pf", [128, 1024], BF16)):
            ps[name] = es.enter_context(nc.psum_tensor(name, shape, dt))

        IB = A["ident_bf"]
        IF = A["ident_f32"]

        def ph_setup(c: Ctx):
            c.do("sync", lambda e: e.dma_start(A["ident_bf"][:], ident_bf_d[:]), inc="dL", label="su_ib")
            c.do("sync", lambda e: e.dma_start(A["ident_f32"][:], ident_f32_d[:]), inc="dL", label="su_if")
            c.do("sync", lambda e: e.dma_start(A["onesrow"][:], onesrow_d[:]), inc="dL", label="su_or")
            for fc in range(4):
                c.do("sync", lambda e, fc=fc: e.dma_start(
                    A["scale_buf"][:, fc // 2, (fc % 2) * 512:(fc % 2) * 512 + 512],
                    memd[fc * 128:(fc + 1) * 128, :]), inc="dL", label=f"su_md{fc}")
            c.do("sync", lambda e: e.dma_start(A["sc1"][0:1, 0:1], memw[:, :]), inc="dL", label="su_mw")
            c.do("sync", lambda e: e.dma_start(
                A["rdw"][:, 0:4], bass.AP(memn, 0, [[1, 128], [128, 4]]),
                allow_slow_non_contiguous=True), inc="dL", label="su_mn")
            c.do("sync", lambda e: e.dma_start(
                A["memnrow"][0:1, :], bass.AP(memn, 0, [[512, 1], [1, 512]])),
                inc="dL", label="su_mnr")

            c.do("vector", lambda e: e.memset(A["cols"][:], 0.0))
            c.do("vector", lambda e: e.memset(A["sc1"][0:1, 1:8], 0.0))
            c.do("vector", lambda e: e.memset(A["ones2"][:], 0.0))
            c.do("vector", lambda e: e.memset(A["ones2"][0:64, 0:1], 1.0))
            c.do("vector", lambda e: e.memset(A["ones2"][64:128, 1:2], 1.0))
            c.do("vector", lambda e: e.memset(A["onescol"][:], 1.0))
            c.do("vector", lambda e: e.memset(A["onesT"][:], 1.0))
            c.do("vector", lambda e: e.memset(A["hmax"][:], 0.0))
            c.do("vector", lambda e: e.memset(A["sc_cfc"][:], 0.0))
            c.do("vector", lambda e: e.memset(A["sc_cp"][:], 0.0))
            c.do("vector", lambda e: e.memset(
                bass.AP(A["v_tm"], 64, [[16 * 520, 128], [520, 16], [65, 8]]), 1.0))
            c.wge("vector", "dL", "su_mnr")
            for fc in range(4):
                c.do("vector", lambda e, fc=fc: e.tensor_copy(
                    A["smem"][:, fc, :],
                    A["scale_buf"][:, fc // 2, (fc % 2) * 512:(fc % 2) * 512 + 512]))
            c.do("vector", lambda e: e.tensor_copy(
                bass.AP(A["smemn"], 0, [[4, 128], [1, 4]]), A["rdw"][:, 0:4]),
                inc="V", label="su_v")
            c.wge("scalar", "dL", "su_mnr")
            c.do("scalar", lambda e: e.activation(A["sc1"][0:1, 1:2], A["sc1"][0:1, 0:1], ACT.Sigmoid),
                 inc="S", label="su_sig")
            c.wge("vector", "S", "su_sig")
            c.do("vector", lambda e: e.tensor_scalar(A["sc1"][0:1, 2:3], A["sc1"][0:1, 1:2],
                                                     -1.0, 1.0, ALU.mult, ALU.add),
                 inc="V", label="su_ow")
            c.wge("tensor", "V", "su_ow")
            c.wge("tensor", "dL", "su_or")
            c.wge("tensor", "dL", "su_mnr")
            for fc in range(4):
                c.do("tensor", lambda e, fc=fc: e.matmul(
                    ps["pc"][:, fc * 128:(fc + 1) * 128], A["memnrow"][0:1, fc * 128:(fc + 1) * 128],
                    A["onesrow"][:, :], start=True, stop=True),
                    inc="T" if fc == 3 else None, label="su_rep" if fc == 3 else None)
            c.wge("vector", "T", "su_rep")
            c.do("vector", lambda e: e.tensor_copy(
                A["smemn_rep"][:, :, :].rearrange("p a b -> p (a b)"), ps["pc"][:, 0:512]),
                inc="V", label="su_repv")
            c.do("tensor", lambda e: e.matmul(ps["pe"][:, 0:1], A["onesrow"][:, :],
                                              A["sc1"][0:1, 1:2], start=True, stop=True))
            c.do("tensor", lambda e: e.matmul(ps["pe"][:, 1:2], A["onesrow"][:, :],
                                              A["sc1"][0:1, 2:3], start=True, stop=True),
                 inc="T", label="su_bc")
            c.wge("vector", "T", "su_bc")
            c.do("vector", lambda e: e.tensor_copy(A["cols"][:, 0:2], ps["pe"][:, 0:2]))
            c.vsep(A)
            c.do("vector", lambda e: e.tensor_scalar(
                A["cols"][:, 14:15], A["cols"][:, 1:2], 1099511627776.0, None, ALU.mult),
                 inc="V", label="su_cols")

        def ph_p0(c: Ctx):
            for i in range(16):
                own = i < 8
                buf = i % 2
                src = x_own if own else x_par
                r0 = (i % 8) * 128
                if i < 2:
                    c.wge("sync", "V", "su_v")
                else:
                    c.wge("sync", "S", f"p0_ln_{i-2}")
                c.do("sync", lambda e, src=src, r0=r0, buf=buf: e.dma_start(
                    A["x_tile"][:, buf, :], src[r0:r0 + 128, :]), inc="dL", label=f"p0_ld_{i}")
                c.wge("vector", "dL", f"p0_ld_{i}")
                c.do("vector", lambda e, buf=buf: e.reduce_sum(
                    A["stats"][:, 0:1], A["x_tile"][:, buf, :], axis=AX.X), inc="V",
                    label=f"p0_sx_{i}")
                c.prot(A, f"p0_sxp_{i}")
                c.do("vector", lambda e, buf=buf: e.scalar_tensor_tensor(
                    A["ln_junk"][:], A["x_tile"][:, buf, :], 1.0, A["x_tile"][:, buf, :],
                    ALU.mult, ALU.mult, accum_out=A["stats"][:, 1:2]),
                    inc="V", label=f"p0_sxx_{i}")
                c.prot(A, f"p0_sxxp_{i}")
                c.do("vector", lambda e: e.tensor_scalar(A["stats"][:, 2:3], A["stats"][:, 0:1],
                                                         1.0 / D, None, ALU.mult))
                c.vsep(A)
                c.do("vector", lambda e: e.tensor_mul(A["stats"][:, 3:4], A["stats"][:, 2:3],
                                                      A["stats"][:, 2:3]))
                c.vsep(A)
                c.do("vector", lambda e: e.scalar_tensor_tensor(
                    A["stats"][:, 4:5], A["stats"][:, 1:2], 1.0 / D, A["stats"][:, 3:4],
                    ALU.mult, ALU.subtract))
                c.vsep(A)
                c.do("vector", lambda e: e.tensor_scalar(A["stats"][:, 4:5], A["stats"][:, 4:5],
                                                         EPS_LN, None, ALU.add),
                     inc="V", label=f"p0_var_{i}")
                c.wge("scalar", "V", f"p0_var_{i}")
                c.do("scalar", lambda e: e.activation(A["stats"][:, 5:6], A["stats"][:, 4:5],
                                                      ACT.Sqrt), inc="S", label=f"p0_sd_{i}")
                c.wge("vector", "S", f"p0_sd_{i}")
                c.do("vector", lambda e: e.reciprocal(A["stats"][:, 6:7], A["stats"][:, 5:6]))
                c.prot(A, f"p0_rcp_{i}")
                c.do("vector", lambda e: e.scalar_tensor_tensor(
                    A["stats"][:, 7:8], A["stats"][:, 2:3], -1.0, A["stats"][:, 6:7],
                    ALU.mult, ALU.mult), inc="V", label=f"p0_rb_{i}")
                c.wge("scalar", "V", f"p0_rb_{i}")
                if i >= 1:
                    c.wge("scalar", "T", f"p0_tr_{i-1}_3")
                c.do("scalar", lambda e, buf=buf: e.activation(
                    A["ptA"][:, 0:2048], A["x_tile"][:, buf, :], ACT.Identity,
                    bias=A["stats"][:, 7:8], scale=A["stats"][:, 6:7]),
                    inc="S", label=f"p0_ln_{i}")
                if DEBUG:
                    c.wge("vector", "S", f"p0_ln_{i}")
                    c.do("vector", lambda e, i=i: e.tensor_copy(
                        A["shist"][:, i * 8:(i + 1) * 8], A["stats"][:, 0:8]))
                dst = A["xn_own"] if own else A["xn_par"]
                tcol = (i % 8) * 128
                for grp in range(4):
                    if grp == 0:
                        c.wge("tensor", "S", f"p0_ln_{i}")
                        if i == 0:
                            c.wge("tensor", "dL", "su_mnr")
                        else:
                            c.wge("tensor", "V", f"p0_ev_{i-1}_3")
                    else:
                        c.wge("tensor", "V", f"p0_ev_{i}_{grp-1}")
                    for dd in range(4):
                        d = grp * 4 + dd
                        c.do("tensor", lambda e, d=d, dd=dd: e.transpose(
                            ps["pf"][:, dd * 128:(dd + 1) * 128],
                            A["ptA"][:, d * 128:(d + 1) * 128], IB[:]),
                            inc="T" if dd == 3 else None,
                            label=f"p0_tr_{i}_{grp}" if dd == 3 else None)
                    c.wge("vector", "T", f"p0_tr_{i}_{grp}")
                    for dd in range(4):
                        d = grp * 4 + dd
                        c.do("vector", lambda e, d=d, dd=dd, dst=dst, tcol=tcol: e.tensor_copy(
                            dst[:, d, tcol:tcol + 128], ps["pf"][:, dd * 128:(dd + 1) * 128]),
                            inc="V" if dd == 3 else None,
                            label=f"p0_ev_{i}_{grp}" if dd == 3 else None)

        def w_pipeline(c: Ctx, wsrc, n_ic, kind, prev_pf_label, prev_wbuf_waits):
            last_mm = {"q": lambda ic: f"p1q_mm_{ic}_1",
                       "k": lambda ic: f"p1k_mm_{ic}_1_1",
                       "v": lambda ic: f"p1v_mm_{ic}_1_1"}[kind]
            for ic in range(n_ic):
                buf = ic % 2
                if ic < 2:
                    for (weng, wsem, wlab) in prev_wbuf_waits:
                        c.wge("sync", wsem, wlab)
                else:
                    c.wge("sync", "V", f"p1{kind}_cast_{ic-2}")
                c.do("sync", lambda e, ic=ic, buf=buf, wsrc=wsrc: e.dma_start(
                    A["w_buf"][:, buf, :], wsrc[ic * 128:(ic + 1) * 128, :]),
                    inc="dL", label=f"p1{kind}_ld_{ic}")
                c.wge("vector", "dL", f"p1{kind}_ld_{ic}")
                c.do("vector", lambda e, buf=buf: e.tensor_copy(
                    A["ptB"][:, 0:2048], A["w_buf"][:, buf, :]),
                    inc="V", label=f"p1{kind}_cast_{ic}")
                if P1_SUB < 2:
                    continue
                for grp in range(4):
                    if grp == 0:
                        c.wge("tensor", "V", f"p1{kind}_cast_{ic}")
                        c.wge("tensor", "V", prev_pf_label if ic == 0 else f"p1{kind}_wev_{ic-1}_3")
                    else:
                        c.wge("tensor", "V", f"p1{kind}_wev_{ic}_{grp-1}")
                    for dd in range(4):
                        d = grp * 4 + dd
                        c.do("tensor", lambda e, d=d, dd=dd: e.transpose(
                            ps["pf"][:, dd * 128:(dd + 1) * 128],
                            A["ptB"][:, d * 128:(d + 1) * 128], IB[:]),
                            inc="T" if dd == 3 else None,
                            label=f"p1{kind}_wtr_{ic}_{grp}" if dd == 3 else None)
                    c.wge("vector", "T", f"p1{kind}_wtr_{ic}_{grp}")
                    if grp == 0 and ic >= 1:
                        c.wge("vector", "T", last_mm(ic - 1))
                    for dd in range(4):
                        d = grp * 4 + dd
                        c.do("vector", lambda e, d=d, dd=dd: e.tensor_copy(
                            A["wT_buf"][:, d, :], ps["pf"][:, dd * 128:(dd + 1) * 128]),
                            inc="V" if dd == 3 else None,
                            label=f"p1{kind}_wev_{ic}_{grp}" if dd == 3 else None)
                if P1_SUB < 3:
                    continue
                halves = [("own", A["xn_own"])] if kind == "q" else \
                         [("own", A["xn_own"]), ("par", A["xn_par"])]
                for hi, (hn, xsrc) in enumerate(halves):
                    for tch in range(2):
                        pdst = ps["pc"] if tch == 0 else ps["pd"]
                        mlabel = (f"p1q_mm_{ic}_{tch}" if kind == "q"
                                  else f"p1{kind}_mm_{ic}_{hi}_{tch}")
                        for d in range(16):
                            if d == 0 and hi == 0 and tch == 0:
                                c.wge("tensor", "V", f"p1{kind}_wev_{ic}_3")
                            if d == 0:
                                prev = None
                                if kind == "q" and ic > 0:
                                    prev = f"p1q_nmm_{ic-1}_{tch}"
                                if kind != "q" and (ic > 0 or hi > 0):
                                    ph_, pic = (hi - 1, ic) if hi > 0 else (1, ic - 1)
                                    prev = (f"p1k_nmm_{pic}_{ph_}_{tch}" if kind == "k"
                                            else f"p1v_evd_{pic}_{ph_}_{tch}")
                                if prev is not None:
                                    c.wge("tensor", "T", prev)
                            c.do("tensor", lambda e, d=d, tch=tch, pdst=pdst, xsrc=xsrc: e.matmul(
                                pdst[:, 0:512], A["wT_buf"][:, d, :],
                                xsrc[:, d, tch * 512:(tch + 1) * 512],
                                start=(d == 0), stop=(d == 15)),
                                inc="T" if d == 15 else None,
                                label=mlabel if d == 15 else None)
                        if P1_SUB < 4:
                            continue
                        if kind == "q":
                            c.wge("vector", "T", mlabel)
                            c.do("vector", lambda e, ic=ic, tch=tch, pdst=pdst: e.tensor_copy(
                                A["q_fm"][:, ic, tch * 512:(tch + 1) * 512], pdst[:, 0:512]),
                                inc="V", label=f"p1q_ev_{ic}_{tch}")
                            if P1_SUB < 5:
                                continue
                            if SQ_SBUF:
                                c.wge("scalar", "V", f"p1q_ev_{ic}_{tch}")
                                c.do("scalar", lambda e, ic=ic, tch=tch: e.activation(
                                    A["sq_sb"][:, tch * 512:(tch + 1) * 512],
                                    A["q_fm"][:, ic, tch * 512:(tch + 1) * 512],
                                    ACT.Square), inc="S", label=f"p1q_sq_{ic}_{tch}")
                            else:
                                c.wge("scalar", "T", mlabel)
                                c.do("scalar", lambda e, tch=tch, pdst=pdst: e.activation(
                                    A["sq_sb"][:, tch * 512:(tch + 1) * 512], pdst[:, 0:512],
                                    ACT.Square), inc="S", label=f"p1q_sq_{ic}_{tch}")
                            if P1_SUB < 6:
                                continue
                            c.wge("tensor", "S", f"p1q_sq_{ic}_{tch}")
                            c.wge("tensor", "V", f"p1q_ev_{ic}_{tch}")
                            if ic > 0 or tch > 0:
                                c.wge("tensor", "V",
                                      f"p1q_nred_{ic}_{tch-1}" if tch > 0 else f"p1q_nred_{ic-1}_1")
                            c.do("tensor", lambda e, tch=tch: e.matmul(
                                ps["pa"][0:1, 0:512], A["ones2"][:, 0:1],
                                A["sq_sb"][:, tch * 512:(tch + 1) * 512], start=True, stop=True))
                            c.do("tensor", lambda e, tch=tch: e.matmul(
                                ps["pa"][0:1, 512:1024], A["ones2"][:, 1:2],
                                A["sq_sb"][:, tch * 512:(tch + 1) * 512], start=True, stop=True),
                                inc="T", label=f"p1q_nmm_{ic}_{tch}")
                            if P1_SUB >= 7:
                                c.wge("vector", "T", f"p1q_nmm_{ic}_{tch}")
                                c.do("vector", lambda e, ic=ic, tch=tch: e.reduce_max(
                                    A["nq_accf"][0:1, 2 * ic + tch:2 * ic + tch + 1],
                                    ps["pa"][0:1, 0:1024], axis=AX.X),
                                    inc="V", label=f"p1q_nred_{ic}_{tch}")
                        elif kind == "k":
                            col = hi * 1024 + tch * 512
                            c.wge("vector", "T", mlabel)
                            c.do("vector", lambda e, ic=ic, col=col, pdst=pdst: e.tensor_copy(
                                A["k_fm"][:, ic, col:col + 512], pdst[:, 0:512]),
                                inc="V", label=f"p1k_ev_{ic}_{hi}_{tch}")
                            c.wge("scalar", "V", f"p1k_ev_{ic}_{hi}_{tch}")
                            c.do("scalar", lambda e, ic=ic, col=col, tch=tch: e.activation(
                                A["sq_sb"][:, tch * 512:(tch + 1) * 512],
                                A["k_fm"][:, ic, col:col + 512],
                                ACT.Square), inc="S", label=f"p1k_sq_{ic}_{hi}_{tch}")
                            c.wge("tensor", "S", f"p1k_sq_{ic}_{hi}_{tch}")
                            c.wge("tensor", "V", f"p1k_ev_{ic}_{hi}_{tch}")
                            c.wge("tensor", "V", "p1q_nred_15_1" if (ic == 0 and hi == 0 and tch == 0)
                                  else (f"p1k_nred_{ic}_{hi}_{tch-1}" if tch > 0
                                        else (f"p1k_nred_{ic}_{hi-1}_1" if hi > 0
                                              else f"p1k_nred_{ic-1}_1_1")))
                            c.do("tensor", lambda e, tch=tch: e.matmul(
                                ps["pa"][0:1, 0:512], A["ones2"][:, 0:1],
                                A["sq_sb"][:, tch * 512:(tch + 1) * 512], start=True, stop=True))
                            c.do("tensor", lambda e, tch=tch: e.matmul(
                                ps["pa"][0:1, 512:1024], A["ones2"][:, 1:2],
                                A["sq_sb"][:, tch * 512:(tch + 1) * 512], start=True, stop=True),
                                inc="T", label=f"p1k_nmm_{ic}_{hi}_{tch}")
                            c.wge("vector", "T", f"p1k_nmm_{ic}_{hi}_{tch}")
                            c.do("vector", lambda e, ic=ic, hi=hi, tch=tch: e.reduce_max(
                                A["nk_accf"][0:1, 4 * ic + 2 * hi + tch:4 * ic + 2 * hi + tch + 1],
                                ps["pa"][0:1, 0:1024], axis=AX.X),
                                inc="V", label=f"p1k_nred_{ic}_{hi}_{tch}")
                        else:
                            c.wge("vector", "T", mlabel)
                            if ic > 0 or hi > 0 or tch > 0:
                                prev_vtr = (f"p1v_vtr_{ic}_{hi}_{tch-1}" if tch > 0
                                            else (f"p1v_vtr_{ic}_{hi-1}_1" if hi > 0
                                                  else f"p1v_vtr_{ic-1}_1_1"))
                                c.wge("vector", "T", prev_vtr)
                            c.do("vector", lambda e, pdst=pdst: e.tensor_copy(
                                A["v_sb"][:, 0:512], pdst[:, 0:512]),
                                inc="V", label=f"p1v_evd_{ic}_{hi}_{tch}")
                            c.wge("tensor", "V", f"p1v_evd_{ic}_{hi}_{tch}")
                            c.wge("tensor", "V", f"p1v_vev_{ic}_{hi}_{tch-1}" if tch > 0
                                  else (f"p1v_vev_{ic}_{hi-1}_1" if hi > 0
                                        else (f"p1v_vev_{ic-1}_1_1" if ic > 0
                                              else f"p1v_wev_{0}_3")))
                            for tt in range(4):
                                c.do("tensor", lambda e, tt=tt: e.transpose(
                                    ps["pf"][:, tt * 128:(tt + 1) * 128],
                                    A["v_sb"][:, tt * 128:(tt + 1) * 128], IB[:]),
                                    inc="T" if tt == 3 else None,
                                    label=f"p1v_vtr_{ic}_{hi}_{tch}" if tt == 3 else None)
                            c.wge("vector", "T", f"p1v_vtr_{ic}_{hi}_{tch}")
                            tok0 = (hi * 1024 + tch * 512) // 128
                            h0, h1 = 2 * ic, 2 * ic + 1
                            for tt in range(4):
                                tb = tok0 + tt
                                c.do("vector", lambda e, tt=tt, tb=tb, h0=h0: e.tensor_copy(
                                    A["v_tm"][:, tb, h0 * 65:h0 * 65 + 64],
                                    ps["pf"][:, tt * 128:tt * 128 + 64]))
                                c.do("vector", lambda e, tt=tt, tb=tb, h1=h1: e.tensor_copy(
                                    A["v_tm"][:, tb, h1 * 65:h1 * 65 + 64],
                                    ps["pf"][:, tt * 128 + 64:tt * 128 + 128]),
                                    inc="V" if tt == 3 else None,
                                    label=f"p1v_vev_{ic}_{hi}_{tch}" if tt == 3 else None)

        def ph_p1(c: Ctx):
            w_pipeline(c, wq, P1_QIC, "q", "p0_ev_15_3",
                       [("sync", "S", "p0_ln_15"), ("sync", "V", "p0_sx_15")])
            if P1_MODE >= 2 or P1_MODE == 0:
                w_pipeline(c, wk, 4, "k", f"p1q_wev_{P1_QIC-1}_3", [("sync", "V", f"p1q_cast_{P1_QIC-1}")])
            if P1_MODE >= 3 or P1_MODE == 0:
                w_pipeline(c, wv, 4, "v", "p1k_wev_3_3", [("sync", "V", "p1k_cast_3")])
            if P1_MODE not in (0, 4):
                return
            c.wge("vector", "V", "p1v_vev_3_1_1")
            c.prot(A, "p1_cpre")
            c.do("vector", lambda e: e.reduce_max(A["sc1"][0:1, 3:4], A["nq_accf"][0:1, :], axis=AX.X))
            c.do("vector", lambda e: e.reduce_max(A["sc1"][0:1, 5:6], A["nk_accf"][0:1, :], axis=AX.X))
            c.prot(A, "p1_cprot")
            c.do("vector", lambda e: e.tensor_mul(A["sc1"][0:1, 7:8], A["sc1"][0:1, 3:4],
                                                  A["sc1"][0:1, 5:6]), inc="V", label="p1_c0")
            c.wge("scalar", "V", "p1_c0")
            c.do("scalar", lambda e: e.activation(A["sc1"][0:1, 7:8], A["sc1"][0:1, 7:8], ACT.Sqrt),
                 inc="S", label="p1_c1")
            c.wge("vector", "S", "p1_c1")
            c.do("vector", lambda e: e.tensor_scalar(A["sc1"][0:1, 7:8], A["sc1"][0:1, 7:8],
                                                     -0.125 * 1.02, None, ALU.mult),
                 inc="V", label="p1_c2")
            c.wge("tensor", "V", "p1_c2")
            c.do("tensor", lambda e: e.matmul(ps["pe"][:, 2:3], A["onesrow"][:, :],
                                              A["sc1"][0:1, 7:8], start=True, stop=True),
                 inc="T", label="p1_c3")
            c.wge("vector", "T", "p1_c3")
            c.do("vector", lambda e: e.tensor_copy(A["cols"][:, 2:3], ps["pe"][:, 2:3]),
                 inc="V", label="p1_bias")

        def ph_p2(c: Ctx):
            def scale_chunk(cn):
                srcw = cfc if cn < 64 else cproj
                k = cn % 64
                r, cc = k // 4, k % 4
                buf = cn % 2
                acc = A["sc_cfc"] if cn < 64 else A["sc_cp"]
                if cn < 2:
                    c.wge("sync", "V", "p1v_cast_3")
                else:
                    c.wge("sync", "V", f"p2sc_red_{cn-2}")
                c.do("sync", lambda e, srcw=srcw, r=r, cc=cc, buf=buf: e.dma_start(
                    A["scale_buf"][:, buf, :],
                    srcw[r * 128:(r + 1) * 128, cc * 2048:(cc + 1) * 2048]),
                    inc="dL", label=f"p2sc_ld_{cn}")
                c.wge("vector", "dL", f"p2sc_ld_{cn}")
                c.do("vector", lambda e, buf=buf, acc=acc, k=k: e.tensor_reduce(
                    acc[:, k:k + 1], A["scale_buf"][:, buf, :], axis=AX.X, op=ALU.add,
                    apply_absolute_value=True), inc="V", label=f"p2sc_red_{cn}")

            for u in range(8):
                g, sc = u // 2, u % 2
                if SCALE_INLINE:
                    for jj in range(16):
                        scale_chunk(u * 16 + jj)
                if sc == 0:
                    for fc in range(4):
                        qt = g * 4 + fc
                        if g >= 1 and fc == 0:
                            c.wge("vector", "T", f"p2_am_{2*g-1}_3")
                            c.wge("vector", "T", f"p2_pv_{2*g-1}_3_1")
                            if DEBUG2:
                                c.wge("vector", "dG", f"p2_eldmp_{g-1}")
                        c.do("vector", lambda e, qt=qt: e.tensor_scalar(
                            A["ptB"][:, 0:1024], A["q_fm"][:, qt, :], 0.0, None, ALU.min),
                            inc="V", label=f"p2_elmin_{g}_{fc}")
                        c.wge("scalar", "V", f"p2_elmin_{g}_{fc}")
                        c.do("scalar", lambda e: e.activation(
                            A["ptA"][:, 0:1024], A["ptB"][:, 0:1024], ACT.Exp),
                            inc="S", label=f"p2_elexp_{g}_{fc}")
                        c.wge("vector", "S", f"p2_elexp_{g}_{fc}")
                        c.do("vector", lambda e, qt=qt, fc=fc: e.scalar_tensor_tensor(
                            A["eluq"][:, fc, :], A["q_fm"][:, qt, :], 0.0, A["ptA"][:, 0:1024],
                            ALU.max, ALU.add))
                        c.vsep(A)
                        c.do("vector", lambda e, fc=fc: e.tensor_scalar(
                            A["eluq"][:, fc, :], A["eluq"][:, fc, :], -1.0, None, ALU.add),
                            inc="V", label=f"p2_el_{g}_{fc}")
                if DEBUG2 and sc == 0:
                    c.wge("gpsimd", "V", f"p2_el_{g}_3")
                    c.do("gpsimd", lambda e, g=g: e.dma_start(
                        dbg2_el[:, g * 4096:(g + 1) * 4096],
                        A["eluq"][:, :, :].rearrange("p a b -> p (a b)")), inc="dG")
                c.wge("tensor", "V", f"p2_el_{g}_3")
                c.wge("tensor", "V", "su_repv" if u == 0 else f"p2_amev_{u-1}_3")
                for fc in range(4):
                    c.do("tensor", lambda e, fc=fc, sc=sc: e.matmul(
                        ps["pe"][:, 0:512], A["smemn_rep"][:, fc, :],
                        A["eluq"][:, fc, sc * 512:(sc + 1) * 512],
                        start=(fc == 0), stop=(fc == 3)),
                        inc="T" if fc == 3 else None, label=f"p2_nm_{u}" if fc == 3 else None)
                c.wge("vector", "T", f"p2_nm_{u}")
                c.do("vector", lambda e: e.tensor_copy(A["nrow_sb"][:, :], ps["pe"][:, 0:512]))
                c.do("vector", lambda e: e.tensor_copy(A["cols"][:, 12:13], A["cols"][:, 13:14]),
                     inc="V", label=f"p2_nev_{u}")
                c.wge("tensor", "V", f"p2_nev_{u}")
                if u >= 1:
                    c.wge("tensor", "V", f"p2_ambl_{u-1}_3")
                elif True:
                    c.wge("tensor", "V", "p1_bias")
                for tc in range(4):
                    c.do("tensor", lambda e, tc=tc: e.transpose(
                        ps["pf"][:, tc * 128:(tc + 1) * 128],
                        A["nrow_sb"][:, tc * 128:(tc + 1) * 128], IB[:]),
                        inc="T" if tc == 3 else None, label=f"p2_ntr_{u}" if tc == 3 else None)
                c.wge("vector", "T", f"p2_ntr_{u}")
                for tc in range(4):
                    c.do("vector", lambda e, tc=tc: e.reciprocal(
                        A["rnw"][:, tc:tc + 1], ps["pf"][:, tc * 128:tc * 128 + 1]))
                c.prot(A, f"p2_rnwp_{u}")
                c.do("vector", lambda e: e.tensor_scalar(
                    A["rnw"][:, 0:4], A["rnw"][:, 0:4], A["cols"][:, 0:1], None, ALU.mult),
                    inc="V", label=f"p2_rnw_{u}")
                if u >= 1:
                    c.wge("vector", "dG", f"p2_st_{u-1}")
                if DEBUG2:
                    c.wge("gpsimd", "V", f"p2_rnw_{u}")
                    c.do("gpsimd", lambda e, u=u: e.dma_start(
                        dbg2_nrow[:, u * 512:(u + 1) * 512], A["nrow_sb"][:, :]), inc="dG")
                    c.do("gpsimd", lambda e, u=u: e.dma_start(
                        dbg2_rnw[:, u * 4:(u + 1) * 4], A["rnw"][:, 0:4]), inc="dG")

                for hp in range(4):
                    if hp >= 1:
                        c.wge("tensor", "V", f"p2_amev_{u}_{hp-1}")
                    for fc in range(4):
                        c.do("tensor", lambda e, fc=fc, hp=hp, sc=sc: e.matmul(
                            ps["pe"][:, 0:512], A["smem"][:, fc, hp * 128:(hp + 1) * 128],
                            A["eluq"][:, fc, sc * 512:(sc + 1) * 512],
                            start=(fc == 0), stop=(fc == 3)),
                            inc="T" if fc == 3 else None,
                            label=f"p2_am_{u}_{hp}" if fc == 3 else None)
                    c.wge("vector", "T", f"p2_am_{u}_{hp}")
                    if u > 0 or hp > 0:
                        c.wge("vector", "T", f"p2_amtr_{u}_{hp-1}" if hp > 0
                              else f"p2_amtr_{u-1}_3")
                        if DEBUG2:
                            c.wge("vector", "dG", f"p2_amdmp_{u}_{hp-1}" if hp > 0
                                  else f"p2_amdmp_{u-1}_3")
                    c.do("vector", lambda e: e.tensor_copy(A["am_sb"][:, 0:512], ps["pe"][:, 0:512]))
                    c.do("vector", lambda e: e.tensor_copy(A["cols"][:, 12:13], A["cols"][:, 13:14]),
                         inc="V", label=f"p2_amev_{u}_{hp}")

                    qt = g * 4 + hp
                    if u > 0 or hp > 0:
                        for hd in range(2):
                            c.wge("tensor", "V", f"p2_pvev_{u}_{hp-1}_{hd}" if hp > 0
                                  else f"p2_pvev_{u-1}_3_{hd}")
                    for grp in range(8):
                        half = grp % 2
                        for hd in range(2):
                            pq = ps["pa"] if hd == 0 else ps["pb"]
                            plo, phi = (0, 64) if hd == 0 else (64, 128)
                            for cc2 in range(2):
                                tch = grp * 2 + cc2
                                c.do("tensor", lambda e, tch=tch, cc2=cc2, qt=qt, hp=hp,
                                     plo=plo, phi=phi, pq=pq, sc=sc: e.matmul(
                                    pq[:, cc2 * 512:(cc2 + 1) * 512],
                                    A["k_fm"][plo:phi, hp, tch * 128:(tch + 1) * 128],
                                    A["q_fm"][plo:phi, qt, sc * 512:(sc + 1) * 512],
                                    start=True, stop=True),
                                    inc="T" if cc2 == 1 else None,
                                    label=f"p2_qk_{u}_{hp}_{hd}_{grp}" if cc2 == 1 else None)
                        for hd in range(2):
                            pq = ps["pa"] if hd == 0 else ps["pb"]
                            pt_sb = A["ptA"] if hd == 0 else A["ptB"]
                            c.wge("scalar", "T", f"p2_qk_{u}_{hp}_{hd}_{grp}")
                            if u == 0 and hp == 0 and hd == 0 and grp == 0:
                                c.wge("scalar", "V", "p1_bias")
                            c.do("scalar", lambda e, pq=pq, pt_sb=pt_sb, half=half: e.activation(
                                pt_sb[:, half * 1024:(half + 1) * 1024], pq[:, 0:1024], ACT.Exp,
                                bias=A["cols"][:, 2:3], scale=0.125),
                                inc="S", label=f"p2_exp_{u}_{hp}_{hd}_{grp}")
                        for hd in range(2):
                            h = 2 * hp + hd
                            ppv = ps["pc"] if hd == 0 else ps["pd"]
                            pt_sb = A["ptA"] if hd == 0 else A["ptB"]
                            c.wge("tensor", "S", f"p2_exp_{u}_{hp}_{hd}_{grp}")
                            for cc2 in range(2):
                                tch = grp * 2 + cc2
                                c.do("tensor", lambda e, tch=tch, cc2=cc2, h=h, grp=grp,
                                     ppv=ppv, pt_sb=pt_sb, half=half: e.matmul(
                                    ppv[0:65, 0:512], A["v_tm"][:, tch, h * 65:(h + 1) * 65],
                                    pt_sb[:, half * 1024 + cc2 * 512:half * 1024 + (cc2 + 1) * 512],
                                    start=(grp == 0 and cc2 == 0), stop=(grp == 7 and cc2 == 1)),
                                    inc="T" if (grp == 7 and cc2 == 1) else None,
                                    label=f"p2_pv_{u}_{hp}_{hd}" if (grp == 7 and cc2 == 1) else None)
                    for hd in range(2):
                        ppv = ps["pc"] if hd == 0 else ps["pd"]
                        c.wge("vector", "T", f"p2_pv_{u}_{hp}_{hd}")
                        c.do("vector", lambda e, hd=hd, ppv=ppv: e.tensor_copy(
                            A["pv_sb"][0:65, hd * 512:(hd + 1) * 512], ppv[0:65, 0:512]))
                        c.do("vector", lambda e: e.tensor_copy(A["cols"][:, 12:13], A["cols"][:, 13:14]),
                            inc="V", label=f"p2_pvev_{u}_{hp}_{hd}")
                    c.wge("tensor", "V", f"p2_pvev_{u}_{hp}_1")
                    c.wge("tensor", "V", f"p2_rnw_{u}" if hp == 0 else f"p2_ambl_{u}_{hp-1}")
                    for hd in range(2):
                        for tc in range(4):
                            c.do("tensor", lambda e, hd=hd, tc=tc: e.transpose(
                                ps["pf"][:, hd * 512 + tc * 128:hd * 512 + tc * 128 + 65],
                                A["pv_sb"][0:65, hd * 512 + tc * 128:hd * 512 + (tc + 1) * 128],
                                IB[0:65, 0:65]),
                                inc="T" if (hd == 1 and tc == 3) else None,
                                label=f"p2_pvtr_{u}_{hp}" if (hd == 1 and tc == 3) else None)
                    c.wge("vector", "T", f"p2_pvtr_{u}_{hp}")
                    for dc in range(8):
                        c.do("vector", lambda e, dc=dc: e.tensor_copy(
                            A["dtmp"][:, dc:dc + 1], ps["pf"][:, dc * 128 + 64:dc * 128 + 65]))
                    c.do("vector", lambda e: e.tensor_copy(A["cols"][:, 12:13], A["cols"][:, 13:14]),
                        inc="V", label=f"p2_dcp_{u}_{hp}")
                    c.wge("scalar", "V", f"p2_dcp_{u}_{hp}")
                    c.do("scalar", lambda e: e.activation(
                        A["rdw"][:, 0:4], A["dtmp"][:, 0:4], ACT.Ln, bias=0.0, scale=1099511627776.0))
                    c.do("scalar", lambda e: e.activation(
                        A["dtmp"][:, 4:8], A["dtmp"][:, 4:8], ACT.Ln, bias=0.0, scale=1099511627776.0))
                    c.ssep(A)
                    c.do("scalar", lambda e: e.activation(
                        A["rdw8"][:, 0:4], A["rdw"][:, 0:4], ACT.Exp, bias=0.0, scale=-1.0))
                    c.do("scalar", lambda e: e.activation(
                        A["rdw8"][:, 4:8], A["dtmp"][:, 4:8], ACT.Exp, bias=0.0, scale=-1.0))
                    c.do("scalar", lambda e: e.copy(A["cols"][:, 12:13], A["cols"][:, 15:16]),
                        inc="S", label=f"p2_rd_{u}_{hp}")
                    c.wge("vector", "S", f"p2_rd_{u}_{hp}")
                    c.do("vector", lambda e: e.tensor_scalar(
                        A["rdw8"][:, 0:8], A["rdw8"][:, 0:8], A["cols"][:, 14:15], None, ALU.mult))
                    c.vsep(A)
                    for hd in range(2):
                        h = 2 * hp + hd
                        for tc in range(4):
                            bcol = hd * 512 + tc * 128
                            c.do("vector", lambda e, bcol=bcol, tc=tc, h=h, hd=hd: e.tensor_scalar(
                                A["stage"][:, tc, h * 64:h * 64 + 64],
                                ps["pf"][:, bcol:bcol + 64],
                                A["rdw8"][:, hd * 4 + tc:hd * 4 + tc + 1], None, ALU.mult),
                                inc="V" if (hd == 1 and tc == 3) else None,
                                label=f"p2_bl_{u}_{hp}" if (hd == 1 and tc == 3) else None)
                    c.wge("tensor", "V", f"p2_bl_{u}_{hp}")
                    c.wge("tensor", "V", f"p2_amev_{u}_{hp}")
                    for tc in range(4):
                        c.do("tensor", lambda e, tc=tc: e.transpose(
                            ps["pf"][:, tc * 128:(tc + 1) * 128],
                            A["am_sb"][:, tc * 128:(tc + 1) * 128], IB[:]),
                            inc="T" if tc == 3 else None,
                            label=f"p2_amtr_{u}_{hp}" if tc == 3 else None)
                    c.wge("vector", "T", f"p2_amtr_{u}_{hp}")
                    for tc in range(4):
                        c.do("vector", lambda e, tc=tc, hp=hp: e.scalar_tensor_tensor(
                            A["stage"][:, tc, hp * 128:(hp + 1) * 128],
                            ps["pf"][:, tc * 128:(tc + 1) * 128],
                            A["rnw"][:, tc:tc + 1],
                            A["stage"][:, tc, hp * 128:(hp + 1) * 128],
                            ALU.mult, ALU.add))
                    c.do("vector", lambda e: e.tensor_copy(A["cols"][:, 12:13], A["cols"][:, 13:14]),
                        inc="V", label=f"p2_ambl_{u}_{hp}")
                    if DEBUG2:
                        uh = u * 4 + hp
                        c.wge("gpsimd", "V", f"p2_amev_{u}_{hp}")
                        c.do("gpsimd", lambda e, uh=uh: e.dma_start(
                            dbg2_am[:, uh * 512:(uh + 1) * 512], A["am_sb"][:, 0:512]), inc="dG",
                            label=f"p2_amdmp_{u}_{hp}")
                        c.wge("gpsimd", "V", f"p2_pvev_{u}_{hp}_1")
                        c.do("gpsimd", lambda e, uh=uh: e.dma_start(
                            dbg2_pv[:, uh * 1024:(uh + 1) * 1024], A["pv_sb"][0:65, :]), inc="dG")
                        c.wge("gpsimd", "V", f"p2_bl_{u}_{hp}")
                        c.do("gpsimd", lambda e, uh=uh: e.dma_start(
                            dbg2_rd[:, uh * 8:(uh + 1) * 8], A["rdw8"][:, 0:8]), inc="dG")
                if DEBUG2 and sc == 1:
                    c.wge("gpsimd", "T", f"p2_am_{u}_3")
                    c.do("gpsimd", lambda e, g=g: e.dma_start(
                        dbg2_el2[:, g * 4096:(g + 1) * 4096],
                        A["eluq"][:, :, :].rearrange("p a b -> p (a b)")), inc="dG",
                        label=f"p2_eldmp_{g}")
                c.wge("gpsimd", "V", f"p2_ambl_{u}_3")
                c.do("gpsimd", lambda e, g=g, sc=sc: e.dma_start(
                    bass.AP(attn_scr, (sc * 512) * D + g * 512,
                            [[D, 128], [128 * D, 4], [1, 512]]),
                    A["stage"][:, :, :]), inc="dG", label=f"p2_st_{u}")

            if not SCALE_INLINE:
                for cn in range(128):
                    scale_chunk(cn)
            c.wge("vector", "V", "p2sc_red_127")
            c.prot(A, "p2_scpre")
            c.do("vector", lambda e: e.reduce_sum(A["sc_cfc"][:, 0:1], A["sc_cfc"][:, :], axis=AX.X))
            c.do("vector", lambda e: e.reduce_sum(A["sc_cp"][:, 0:1], A["sc_cp"][:, :], axis=AX.X))
            c.prot(A, "p2_scred")
            c.wge("tensor", "V", "p2_scred")
            c.wge("tensor", "V", "p2_amev_7_3")
            c.do("tensor", lambda e: e.matmul(ps["pe"][0:1, 3:4], A["onescol"][:, :],
                                              A["sc_cfc"][:, 0:1], start=True, stop=True))
            c.do("tensor", lambda e: e.matmul(ps["pe"][0:1, 4:5], A["onescol"][:, :],
                                              A["sc_cp"][:, 0:1], start=True, stop=True),
                 inc="T", label="p2_scmm")
            c.wge("vector", "T", "p2_scmm")
            c.do("vector", lambda e: e.tensor_scalar(A["sc1"][0:1, 0:1], ps["pe"][0:1, 3:4],
                                                     1.0 / (D * H), EPS_Q, ALU.mult, ALU.max))
            c.do("vector", lambda e: e.tensor_scalar(A["sc1"][0:1, 1:2], ps["pe"][0:1, 4:5],
                                                     1.0 / (D * H), EPS_Q, ALU.mult, ALU.max),
                 inc="V", label="p2_s12")
            c.wge("tensor", "V", "p2_s12")
            c.do("tensor", lambda e: e.matmul(ps["pe"][:, 5:6], A["onesrow"][:, :],
                                              A["sc1"][0:1, 0:1], start=True, stop=True))
            c.do("tensor", lambda e: e.matmul(ps["pe"][:, 6:7], A["onesrow"][:, :],
                                              A["sc1"][0:1, 1:2], start=True, stop=True),
                 inc="T", label="p2_sbc")
            c.wge("vector", "T", "p2_sbc")
            c.do("vector", lambda e: e.tensor_copy(A["cols"][:, 10:11], ps["pe"][:, 5:6]))
            c.do("vector", lambda e: e.tensor_copy(A["cols"][:, 11:12], ps["pe"][:, 6:7]))
            c.do("vector", lambda e: e.tensor_scalar(A["cols"][:, 3:4], A["cols"][:, 10:11],
                                                     0.5, None, ALU.mult))
            c.do("vector", lambda e: e.tensor_scalar(A["cols"][:, 4:5], A["cols"][:, 10:11],
                                                     -0.5, None, ALU.mult))
            c.do("vector", lambda e: e.tensor_scalar(A["cols"][:, 5:6], A["cols"][:, 11:12],
                                                     0.5, None, ALU.mult))
            c.do("vector", lambda e: e.tensor_scalar(A["cols"][:, 6:7], A["cols"][:, 11:12],
                                                     -0.5, None, ALU.mult))
            c.do("vector", lambda e: e.tensor_mul(A["cols"][:, 7:8], A["cols"][:, 10:11],
                                                  A["cols"][:, 10:11]))
            c.vsep(A)
            c.do("vector", lambda e: e.tensor_scalar(A["cols"][:, 8:9], A["cols"][:, 7:8],
                                                     NSTEPS, None, ALU.mult))
            c.do("vector", lambda e: e.tensor_scalar(A["cols"][:, 9:10], A["cols"][:, 11:12],
                                                     1.0 / NSTEPS, None, ALU.mult),
                 inc="V", label="p2_cols")

        def ph_p3(c: Ctx):
            for j in range(64):
                buf = j % 2
                if j < 2:
                    c.wge("sync", "V", "p2sc_red_127")
                else:
                    c.wge("sync", "V", f"p3_q1_{j-2}")
                c.do("sync", lambda e, j=j, buf=buf: e.dma_start(
                    A["cfc_buf"][:, buf, :, :],
                    bass.AP(cfc, j * 128, [[H, 128], [128 * H, 16], [1, 128]])),
                    inc="dL", label=f"p3_ld_{j}")
                c.wge("vector", "dL", f"p3_ld_{j}")
                if j == 0:
                    c.wge("vector", "V", "p2_cols")
                if j >= 2:
                    c.wge("vector", "T", f"p3_mm_{j-2}_1")
                c.do("vector", lambda e, buf=buf: e.tensor_scalar(
                    A["q1_buf"][:, buf, :, :], A["cfc_buf"][:, buf, :, :],
                    A["cols"][:, 4:5], None, ALU.is_le))
                c.vsep(A)
                c.do("vector", lambda e, buf=buf: e.scalar_tensor_tensor(
                    A["q1_buf"][:, buf, :, :], A["cfc_buf"][:, buf, :, :],
                    A["cols"][:, 3:4], A["q1_buf"][:, buf, :, :],
                    ALU.is_ge, ALU.subtract), inc="V", label=f"p3_q1_{j}")
                c.wge("tensor", "V", f"p3_q1_{j}")
                if j >= 1:
                    c.wge("tensor", "S", f"p3_ev_{j-1}_1")
                for tch in range(2):
                    pdst = ps["pa"] if tch == 0 else ps["pb"]
                    for d in range(16):
                        c.do("tensor", lambda e, d=d, tch=tch, buf=buf, pdst=pdst: e.matmul(
                            pdst[:, 0:512], A["q1_buf"][:, buf, d, :],
                            A["xn_own"][:, d, tch * 512:(tch + 1) * 512],
                            start=(d == 0), stop=(d == 15)),
                            inc="T" if d == 15 else None,
                            label=f"p3_mm_{j}_{tch}" if d == 15 else None)
                for tch in range(2):
                    pdst = ps["pa"] if tch == 0 else ps["pb"]
                    c.wge("scalar", "T", f"p3_mm_{j}_{tch}")
                    c.do("scalar", lambda e, j=j, tch=tch, pdst=pdst: e.activation(
                        A["h_st"][:, j, tch * 512:(tch + 1) * 512], pdst[:, 0:512], ACT.Relu),
                        inc="S", label=f"p3_ev_{j}_{tch}")
                c.wge("vector", "S", f"p3_ev_{j}_1")
                c.do("vector", lambda e, j=j: e.tensor_max(
                    A["hmax"][:, 0:1024], A["hmax"][:, 0:1024], A["h_st"][:, j, :]),
                    inc="V", label=f"p3_hm_{j}")

            c.wge("tensor", "V", "p3_hm_63")
            c.wge("tensor", "V", "p2_ambl_7_3")
            for tb in range(8):
                c.do("tensor", lambda e, tb=tb: e.transpose(
                    ps["pf"][:, (tb % 8) * 128:(tb % 8) * 128 + 128],
                    A["hmax"][:, tb * 128:(tb + 1) * 128], IB[:]),
                    inc="T" if tb == 7 else None, label="p3_hmtr" if tb == 7 else None)
            c.wge("vector", "T", "p3_hmtr")
            for tb in range(8):
                c.do("vector", lambda e, tb=tb: e.reduce_max(
                    A["rmax_tok"][:, tb:tb + 1], ps["pf"][:, tb * 128:(tb + 1) * 128], axis=AX.X))
            c.prot(A, "p3_redp")
            c.do("vector", lambda e: e.tensor_mul(A["oscale"][:, 0:8], A["rmax_tok"][:, 0:8],
                                                  A["rmax_tok"][:, 0:8]))
            c.vsep(A)
            c.do("vector", lambda e: e.tensor_scalar(A["oscale"][:, 0:8], A["oscale"][:, 0:8],
                                                     A["cols"][:, 7:8], EPS_H, ALU.mult, ALU.max))
            c.vsep(A)
            c.do("vector", lambda e: e.reciprocal(A["rmax_tok"][:, 0:8], A["oscale"][:, 0:8]))
            c.prot(A, "p3_rcpp")
            c.do("vector", lambda e: e.tensor_scalar(A["rmax_tok"][:, 0:8], A["rmax_tok"][:, 0:8],
                                                     A["cols"][:, 8:9], None, ALU.mult))
            c.do("vector", lambda e: e.tensor_scalar(A["oscale"][:, 0:8], A["oscale"][:, 0:8],
                                                     A["cols"][:, 9:10], None, ALU.mult),
                 inc="V", label="p3_mtok")
            for tc in range(8):
                c.do("vector", lambda e, tc=tc: e.tensor_scalar(
                    A["ln_junk"][:, tc * 128:(tc + 1) * 128], A["onesT"][:, :],
                    A["rmax_tok"][:, tc:tc + 1], None, ALU.mult),
                    inc="V" if tc == 7 else None, label="p3_mrep" if tc == 7 else None)
            c.wge("tensor", "V", "p3_mrep")
            c.wge("tensor", "S", "p3_ev_63_1")
            for tc in range(8):
                c.do("tensor", lambda e, tc=tc: e.transpose(
                    ps["pa"][:, (tc % 2) * 512 + (tc // 2) * 128:(tc % 2) * 512 + (tc // 2) * 128 + 128]
                    if False else ps["pa"][:, tc * 128:tc * 128 + 128],
                    A["ln_junk"][:, tc * 128:(tc + 1) * 128], IF[:]),
                    inc="T" if tc == 7 else None, label="p3_mbc" if tc == 7 else None)
            c.wge("vector", "T", "p3_mbc")
            c.do("vector", lambda e: e.tensor_copy(A["m_bcast"][:, 0:1024], ps["pa"][:, 0:1024]),
                 inc="V", label="p3_mb")

            for j in range(64):
                if j == 0:
                    c.wge("scalar", "V", "p3_mb")
                else:
                    c.wge("scalar", "V", f"p3_rq_{j-1}")
                c.do("scalar", lambda e, j=j: e.activation(
                    A["sqh"][:, 0:1024], A["h_st"][:, j, :], ACT.Square),
                    inc="S", label=f"p3_sq_{j}")
                c.wge("vector", "S", f"p3_sq_{j}")
                c.do("vector", lambda e: e.tensor_mul(
                    A["i32h"][:, 0:1024], A["sqh"][:, 0:1024], A["m_bcast"][:, 0:1024]))
                c.vsep(A)
                c.do("vector", lambda e, j=j: e.tensor_scalar(
                    A["h_st"][:, j, :], A["i32h"][:, 0:1024], 0.0, None, ALU.add),
                    inc="V", label=f"p3_rq_{j}")

        def ph_p4a(c: Ctx):
            for ic in range(16):
                for s in range(16):
                    buf = s % 2
                    if ic == 0 and s < 2:
                        c.wge("sync", "V", "p3_q1_63")
                    else:
                        c.wge("sync", "V", f"p4_q2a_{ic}_{s-2}" if s >= 2
                              else f"p4_q2a_{ic-1}_{14 + s}")
                    c.do("sync", lambda e, ic=ic, s=s, buf=buf: e.dma_start(
                        A["cp_buf"][:, buf, :],
                        cproj[ic * 128:(ic + 1) * 128, s * 512:(s + 1) * 512]),
                        inc="dL", label=f"p4_ld_{ic}_{s}")
                    c.wge("vector", "dL", f"p4_ld_{ic}_{s}")
                    if ic == 0 and s == 0:
                        c.wge("vector", "V", "p3_rq_63")
                    if s == 0 and ic > 0:
                        c.wge("vector", "T", f"p4_mm_{ic-1}_1")
                    c.do("vector", lambda e, buf=buf: e.tensor_scalar(
                        A["q2_sl"][:, 0:512], A["cp_buf"][:, buf, :],
                        A["cols"][:, 6:7], None, ALU.is_le))
                    c.vsep(A)
                    c.do("vector", lambda e, buf=buf: e.scalar_tensor_tensor(
                        A["q2_sl"][:, 0:512], A["cp_buf"][:, buf, :],
                        A["cols"][:, 5:6], A["q2_sl"][:, 0:512],
                        ALU.is_ge, ALU.subtract), inc="V", label=f"p4_q2a_{ic}_{s}")
                    c.wge("tensor", "V", f"p4_q2a_{ic}_{s}")
                    if ic > 0 or s > 0:
                        c.wge("tensor", "V", f"p4_q2_{ic}_{s-1}" if s > 0
                              else f"p4_q2_{ic-1}_15")
                        if s == 0:
                            c.wge("tensor", "V", f"p4_osc_{ic-1}_1")
                    elif True:
                        c.wge("tensor", "V", "p3_mb")
                    for tt in range(4):
                        c.do("tensor", lambda e, tt=tt: e.transpose(
                            ps["pf"][:, tt * 128:(tt + 1) * 128],
                            A["q2_sl"][:, tt * 128:(tt + 1) * 128], IB[:]),
                            inc="T" if tt == 3 else None,
                            label=f"p4_wtr_{ic}_{s}" if tt == 3 else None)
                    c.wge("vector", "T", f"p4_wtr_{ic}_{s}")
                    c.do("vector", lambda e, s=s: e.tensor_copy(
                        A["w2T"][:, 4 * s:4 * s + 4, :], ps["pf"][:, 0:512]),
                        inc="V", label=f"p4_q2_{ic}_{s}")
                c.wge("tensor", "V", f"p4_q2_{ic}_15")
                for tch in range(2):
                    pdst = ps["pc"] if tch == 0 else ps["pd"]
                    if ic > 0:
                        c.wge("tensor", "V", f"p4_oev_{ic-1}_{tch}")
                    for j in range(64):
                        c.do("tensor", lambda e, j=j, tch=tch, pdst=pdst: e.matmul(
                            pdst[:, 0:512], A["w2T"][:, j, :],
                            A["h_st"][:, j, tch * 512:(tch + 1) * 512],
                            start=(j == 0), stop=(j == 63)),
                            inc="T" if j == 63 else None,
                            label=f"p4_mm_{ic}_{tch}" if j == 63 else None)
                    c.wge("vector", "T", f"p4_mm_{ic}_{tch}")
                    if ic > 0 or tch > 0:
                        c.wge("vector", "T", f"p4_otr_{ic}_{tch-1}" if tch > 0
                              else f"p4_otr_{ic-1}_1")
                    c.do("vector", lambda e, pdst=pdst: e.tensor_copy(
                        A["o_sb"][:, 0:512], pdst[:, 0:512]), inc="V",
                        label=f"p4_oev_{ic}_{tch}")
                    c.wge("tensor", "V", f"p4_oev_{ic}_{tch}")
                    if ic > 0 or tch > 0:
                        c.wge("tensor", "V", f"p4_osc_{ic}_{tch-1}" if tch > 0
                              else f"p4_osc_{ic-1}_1")
                    for tt in range(4):
                        c.do("tensor", lambda e, tt=tt: e.transpose(
                            ps["pf"][:, 512 + tt * 128:512 + (tt + 1) * 128],
                            A["o_sb"][:, tt * 128:(tt + 1) * 128], IB[:]),
                            inc="T" if tt == 3 else None,
                            label=f"p4_otr_{ic}_{tch}" if tt == 3 else None)
                    c.wge("vector", "T", f"p4_otr_{ic}_{tch}")
                    if ic > 0:
                        c.wge("vector", "dG", f"p4_st_{ic-1}_{tch}")
                    for tt in range(4):
                        tb = tch * 4 + tt
                        c.do("vector", lambda e, tt=tt, tb=tb, tch=tch: e.tensor_scalar(
                            A["ffn_sb"][:, tch, tt * 128:(tt + 1) * 128],
                            ps["pf"][:, 512 + tt * 128:512 + (tt + 1) * 128],
                            A["oscale"][:, tb:tb + 1], None, ALU.mult),
                            inc="V" if tt == 3 else None,
                            label=f"p4_osc_{ic}_{tch}" if tt == 3 else None)
                    c.wge("gpsimd", "V", f"p4_osc_{ic}_{tch}")
                    c.do("gpsimd", lambda e, ic=ic, tch=tch: e.dma_start(
                        bass.AP(ffn_scr, (tch * 512) * D + ic * 128,
                                [[D, 128], [128 * D, 4], [1, 128]]),
                        A["ffn_sb"][:, tch, :]), inc="dG", label=f"p4_st_{ic}_{tch}")

        def ph_p4b(c: Ctx):
            for tb in range(8):
                buf = tb % 2
                if tb >= 2:
                    c.wge("gpsimd", "V", f"p4b_add_{tb-2}")
                if tb == 0:
                    c.wge("gpsimd", "dG", "p4_st_15_1")
                    c.wge("gpsimd", "T", "p4_mm_15_1")
                c.do("gpsimd", lambda e, tb=tb, buf=buf: e.dma_start(
                    A["cmb_x"][:, buf, :], x_own[tb * 128:(tb + 1) * 128, :]), inc="dG",
                    label=f"p4b_lx_{tb}")
                c.do("gpsimd", lambda e, tb=tb, buf=buf: e.dma_start(
                    A["cmb_a"][:, buf, :], attn_scr[tb * 128:(tb + 1) * 128, :]), inc="dG",
                    label=f"p4b_la_{tb}")
                c.do("gpsimd", lambda e, tb=tb, buf=buf: e.dma_start(
                    A["cmb_f"][:, buf, :], ffn_scr[tb * 128:(tb + 1) * 128, :]), inc="dG",
                    label=f"p4b_lf_{tb}")
                c.wge("vector", "dG", f"p4b_lf_{tb}")
                c.do("vector", lambda e, buf=buf: e.tensor_add(
                    A["cmb_o"][:, buf, :], A["cmb_x"][:, buf, :], A["cmb_a"][:, buf, :]))
                c.vsep(A)
                c.do("vector", lambda e, buf=buf: e.tensor_add(
                    A["cmb_o"][:, buf, :], A["cmb_o"][:, buf, :], A["cmb_f"][:, buf, :]),
                    inc="V", label=f"p4b_add_{tb}")
                c.wge("gpsimd", "V", f"p4b_add_{tb}")
                c.do("gpsimd", lambda e, tb=tb, buf=buf: e.dma_start(
                    out_d[tb * 128:(tb + 1) * 128, :], A["cmb_o"][:, buf, :]), inc="dG",
                    label=f"p4b_st_{tb}")
            c.wge("gpsimd", "dG", "p4b_st_7")
            c.wge("gpsimd", "dL", "p4_ld_15_15")

        def ph_dbg(c: Ctx):
            if not DEBUG:
                return
            c.wge("gpsimd", "V", "p1_bias")
            c.do("gpsimd", lambda e: e.dma_start(
                dbg_q[:, :], A["q_fm"][:, :, :].rearrange("p a b -> p (a b)")), inc="dG")
            c.do("gpsimd", lambda e: e.dma_start(
                dbg_k[:, :], A["k_fm"][:, :, :].rearrange("p a b -> p (a b)")), inc="dG")
            c.do("gpsimd", lambda e: e.dma_start(
                dbg_v[:, :], A["v_tm"][:, :, :].rearrange("p a b -> p (a b)")), inc="dG")
            c.do("gpsimd", lambda e: e.dma_start(dbg_c[:, :], A["cols"][:, :]), inc="dG")
            if DEBUG2:
                c.do("gpsimd", lambda e: e.dma_start(
                    dbg2_sm1[:, :], A["smem"][:, :, :].rearrange("p a b -> p (a b)")), inc="dG")
            c.do("gpsimd", lambda e: e.dma_start(dbg_s[:, :], A["sc1"][:, :]), inc="dG")
            c.do("gpsimd", lambda e: e.dma_start(dbg_h[:, :], A["shist"][:, :]), inc="dG")
            c.do("gpsimd", lambda e: e.dma_start(
                dbg_xn[:, :], A["xn_own"][:, :, :].rearrange("p a b -> p (a b)")), inc="dG",
                 label="dbg_done")
            c.wge("gpsimd", "dG", "dbg_done")

        def ph_dbg2(c: Ctx):
            if not DEBUG:
                return
            c.wge("gpsimd", "V", "p2_el_3_3")
            c.wge("gpsimd", "V", "p2_ambl_7_3")
            c.do("gpsimd", lambda e: e.dma_start(
                dbg_e[:, :], A["eluq"][:, :, :].rearrange("p a b -> p (a b)")), inc="dG")
            c.do("gpsimd", lambda e: e.dma_start(dbg_pv[0:65, :], A["pv_sb"][0:65, :]), inc="dG")
            c.do("gpsimd", lambda e: e.dma_start(dbg_rd[:, :], A["rdw8"][:, :]), inc="dG")
            if DEBUG2:
                c.do("gpsimd", lambda e: e.dma_start(
                    dbg2_sm2[:, :], A["smem"][:, :, :].rearrange("p a b -> p (a b)")), inc="dG")
                c.do("gpsimd", lambda e: e.dma_start(
                    dbg2_smr[:, :], A["smemn_rep"][:, :, :].rearrange("p a b -> p (a b)")), inc="dG")
            c.do("gpsimd", lambda e: e.dma_start(dbg_dt[:, :], A["dtmp"][:, :]), inc="dG")
            c.do("gpsimd", lambda e: e.dma_start(dbg_pt[:, :], A["ptB"][:, :]), inc="dG",
                label="dbg2_done")
            c.wge("gpsimd", "dG", "dbg2_done")

        phases = [ph_setup, ph_p0, ph_p1, ph_dbg, ph_p2, ph_dbg2, ph_p3, ph_p4a, ph_p4b]
        if PHASES_N is not None:
            phases = phases[:PHASES_N]

        plan = Ctx("plan", marks=marks, counts=counts)
        for ph in phases:
            ph(plan)

        for ph in phases:
            with nc.Block() as blk:
                @blk.sync
                def _(e, ph=ph):
                    ph(Ctx("sync", e, sems, marks))

                @blk.gpsimd
                def _(e, ph=ph):
                    ph(Ctx("gpsimd", e, sems, marks))

                @blk.tensor
                def _(e, ph=ph):
                    ph(Ctx("tensor", e, sems, marks))

                @blk.vector
                def _(e, ph=ph):
                    ph(Ctx("vector", e, sems, marks))

                @blk.scalar
                def _(e, ph=ph):
                    ph(Ctx("scalar", e, sems, marks))

    return nc


_NC_CACHE = {}


def kernel(L0_attn_x, L0_attn_Wq, L0_attn_Wk, L0_attn_Wv, L0_attn_memory,
           L0_attn_memory_norm, L0_attn_memory_weight, L0_ffn_c_fc, L0_ffn_c_proj):
    x = np.ascontiguousarray(np.asarray(L0_attn_x, dtype=np.float32))
    wq_np = np.ascontiguousarray(np.asarray(L0_attn_Wq, dtype=np.float32))
    wk_np = np.ascontiguousarray(np.asarray(L0_attn_Wk, dtype=np.float32))
    wv_np = np.ascontiguousarray(np.asarray(L0_attn_Wv, dtype=np.float32))
    mem_np = np.ascontiguousarray(np.asarray(L0_attn_memory, dtype=np.float32))
    memn_np = np.ascontiguousarray(np.asarray(L0_attn_memory_norm, dtype=np.float32))
    memw_np = np.asarray(L0_attn_memory_weight, dtype=np.float32).reshape(1, 1)
    cfc_np = np.ascontiguousarray(np.asarray(L0_ffn_c_fc, dtype=np.float32))
    cproj_np = np.ascontiguousarray(np.asarray(L0_ffn_c_proj, dtype=np.float32))

    if "nc" not in _NC_CACHE:
        _NC_CACHE["nc"] = build_program()
    nc = _NC_CACHE["nc"]

    in_maps = []
    for core in range(8):
        b, h = core // 2, core % 2
        in_maps.append({
            "x_own": np.ascontiguousarray(x[b, h * 1024:(h + 1) * 1024, :]),
            "x_par": np.ascontiguousarray(x[b, (1 - h) * 1024:(2 - h) * 1024, :]),
            "wq": wq_np, "wk": wk_np, "wv": wv_np, "memd": mem_np, "memn": memn_np,
            "memw": memw_np, "cfc": cfc_np, "cproj": cproj_np,
        })
    res = run_bass_kernel_spmd(nc, in_maps, core_ids=list(range(8)))
    out = np.empty((4, 2048, 2048), dtype=np.float32)
    for core in range(8):
        b, h = core // 2, core % 2
        out[b, h * 1024:(h + 1) * 1024, :] = np.asarray(res.results[core]["out"])
    return out
